# version 03
# baseline (speedup 1.0000x reference)
import numpy as np
import ml_dtypes
from contextlib import ExitStack

import concourse.bass as bass
import concourse.mybir as mybir
from concourse.bass_utils import run_bass_kernel_spmd

F32 = mybir.dt.float32
BF16 = mybir.dt.bfloat16
U32 = mybir.dt.uint32
AF = mybir.ActivationFunctionType
ALU = mybir.AluOpType
AX = mybir.AxisListType
BF = ml_dtypes.bfloat16

L = 4096
NF = 2049
NII = 86
NH_A = 44
NH_B = NII - NH_A
N_CORES = 8


class Sem:
    def __init__(self, ctx, nc, name):
        self.h = ctx.enter_context(nc.semaphore(name))
        self.n = 0

    def inc(self, inst, k=1):
        inst.then_inc(self.h, k)
        self.n += k
        return self.n


class SemPair:

    def __init__(self, ctx, nc, name, n=2):
        self.s = [Sem(ctx, nc, f"{name}{i}") for i in range(n)]

    def inc(self, i, inst, k=16):
        return self.s[i % len(self.s)].inc(inst, k)

    def h(self, i):
        return self.s[i % len(self.s)].h

    def val(self, i):
        return self.s[i % len(self.s)].n


def _row_halves(rows):
    out, o = [], 0
    while o < rows:
        sz = min(512, rows - o)
        out.append((o, sz))
        o += sz
    return out


_CONST_CACHE = {}


def dft_consts():
    if "D" in _CONST_CACHE:
        return _CONST_CACHE["D"], _CONST_CACHE["Ci"]
    t = np.arange(L, dtype=np.float64)
    f = np.arange(NF, dtype=np.float64)
    ang = 2.0 * np.pi * np.outer(t, f) / L
    D = np.empty((L, L), dtype=np.float64)
    D[:, 0:NF] = np.cos(ang)
    D[:, NF:] = np.sin(ang[:, 1:2048])
    D_t = np.ascontiguousarray(
        D.astype(BF).reshape(32, 128, 32, 128).transpose(2, 1, 0, 3)
    ).reshape(32, 128, 4096)

    lf = np.arange(2176, dtype=np.float64)
    a = np.zeros(2176, dtype=np.float64)
    a[0] = 1.0 / L
    a[2048] = 1.0 / L
    a[1:2048] = 2.0 / L
    li = np.arange(L, dtype=np.float64)
    Ci = (a[:, None] * np.cos(2.0 * np.pi * np.outer(lf, li) / L)).astype(BF)
    Ci[2049:, :] = 0
    Ci_t = np.ascontiguousarray(
        Ci.reshape(17, 128, 8, 512).transpose(2, 1, 0, 3)
    ).reshape(8, 128, 17 * 512)
    _CONST_CACHE["D"] = D_t
    _CONST_CACHE["Ci"] = Ci_t
    return D_t, Ci_t


def build_program(rows, debug=False):
    RB = rows // 128
    RH = _row_halves(rows)
    PSW = rows if rows <= 512 else 1024

    nc = bass.Bass("TRN2", target_bir_lowering=False)

    x_d = nc.dram_tensor("x", [rows, L], F32, kind="ExternalInput").ap()
    u1_d = nc.dram_tensor("U1t", [8, 128, 4096], F32, kind="ExternalInput").ap()
    v1_d = nc.dram_tensor("V1t", [NII, 128, 2048], F32, kind="ExternalInput").ap()
    u2_d = nc.dram_tensor("U2t", [NII, 128, 1024], F32, kind="ExternalInput").ap()
    v2_d = nc.dram_tensor("V2t", [128, 8 * L], F32, kind="ExternalInput").ap()
    b1_d = nc.dram_tensor("b1t", [128, 172], F32, kind="ExternalInput").ap()
    b2_d = nc.dram_tensor("b2", [1, L], F32, kind="ExternalInput").ap()
    dt_d = nc.dram_tensor("Dt", [32, 128, 4096], BF16, kind="ExternalInput").ap()
    ci_d = nc.dram_tensor("Cit", [8, 128, 17 * 512], BF16, kind="ExternalInput").ap()

    out_d = nc.dram_tensor("out", [rows, L], F32, kind="ExternalOutput").ap()

    skind = "ExternalOutput" if debug else "Internal"
    xnd_d = nc.dram_tensor("xnd", [rows, 2 * L], BF16, kind=skind).ap()
    cor_d = nc.dram_tensor("corrd", [L, rows], BF16, kind=skind).ap()
    s_d = nc.dram_tensor("sdram", [rows, L], BF16, kind=skind).ap()
    if debug:
        pdump_d = nc.dram_tensor("pdump", [128, 17 * rows], BF16, kind="ExternalOutput").ap()
        zdump_d = nc.dram_tensor("zdump", [128, 8 * rows], BF16, kind="ExternalOutput").ap()
        e8_d = nc.dram_tensor("e8d", [rows, 8], F32, kind="ExternalOutput").ap()
        d8_d = nc.dram_tensor("d8d", [rows, 8], U32, kind="ExternalOutput").ap()
        w8_d = nc.dram_tensor("w8d", [rows, 8], F32, kind="ExternalOutput").ap()
        pp_d = nc.dram_tensor("ppd", [128, 8 * rows], BF16, kind="ExternalOutput").ap()
        g_dump = nc.dram_tensor("gdump", [rows, L], BF16, kind="ExternalOutput").ap()

    with ExitStack() as CTX:
        def sb(name, shape, dtype, ctx=None):
            return (ctx or CTX).enter_context(nc.sbuf_tensor(name, shape, dtype))

        def ps(name, shape, ctx=None):
            return (ctx or CTX).enter_context(nc.psum_tensor(name, shape, F32))

        s_ld = Sem(CTX, nc, "s_ld")
        g_ld = Sem(CTX, nc, "g_ld")
        s_st = Sem(CTX, nc, "s_st")
        mm = Sem(CTX, nc, "mm")
        lx = SemPair(CTX, nc, "lx")
        dl = SemPair(CTX, nc, "dl")
        ul = SemPair(CTX, nc, "ul")
        cl = SemPair(CTX, nc, "cl")
        ct = SemPair(CTX, nc, "ct")
        gs = SemPair(CTX, nc, "gs")
        vl = SemPair(CTX, nc, "vl")
        us = SemPair(CTX, nc, "us")
        x9 = SemPair(CTX, nc, "x9")
        s9 = Sem(CTX, nc, "s9")
        so = SemPair(CTX, nc, "so")
        ve = Sem(CTX, nc, "ve")
        ac = Sem(CTX, nc, "ac")
        gp = Sem(CTX, nc, "gp")

        P_sb = sb("P_sb", [128, 8 * rows], BF16)
        b1_sb = sb("b1_sb", [128, 172], F32)
        riota = sb("riota", [128, 1], U32)

        with nc.Block() as blk:
            @blk.sync
            def _(sy):
                s_ld.inc(sy.dma_start(out=b1_sb[:], in_=b1_d), 16)

            @blk.gpsimd
            def _(g):
                gp.inc(g.iota(riota[:], [[0, 1]], base=0, channel_multiplier=2 * L))

        with ExitStack() as c1:
            x_t = [sb(f"x_t{i}", [128, L], F32, c1) for i in range(2)]
            xn_t = [sb(f"xn_t{i}", [128, L], BF16, c1) for i in range(2)]
            stats = sb("stats", [128, 48], F32, c1)
            mv = sb("mv", [128, 2], F32, c1)
            veps = sb("veps", [128, 1], F32, c1)
            stdv = sb("stdv", [128, 1], F32, c1)
            rstd = sb("rstd", [128, 1], F32, c1)

            ld0, st0, ve0, ac0 = s_ld.n, s_st.n, ve.n, ac.n
            PPR = 5
            with nc.Block() as blk:
                @blk.sync
                def _(sy):
                    for rb in range(RB):
                        if rb >= 2:
                            sy.wait_ge(ve.h, ve0 + PPR * (rb - 2) + 5)
                            sy.wait_ge(s_st.h, st0 + 32 * rb)
                        lx.inc(rb, sy.dma_start(out=x_t[rb % 2][:],
                                                in_=x_d[rb * 128:(rb + 1) * 128, :]))
                        sy.wait_ge(ve.h, ve0 + PPR * rb + 5)
                        sy.dma_start(out=xnd_d[rb * 128:(rb + 1) * 128, 0:L],
                                     in_=xn_t[rb % 2][:]).then_inc(s_st.h, 16)
                        sy.dma_start(out=xnd_d[rb * 128:(rb + 1) * 128, L:2 * L],
                                     in_=xn_t[rb % 2][:]).then_inc(s_st.h, 16)
                    sy.wait_ge(s_st.h, st0 + 32 * RB)
                s_st.n += 32 * RB

                @blk.vector
                def _(v):
                    for rb in range(RB):
                        v.wait_ge(s_ld.h, ld0)
                        v.wait_ge(lx.h(rb), 16 * (rb // 2 + 1))
                        xt = x_t[rb % 2]
                        for j in range(7):
                            v.bn_stats(stats[:, 6 * j:6 * (j + 1)], xt[:, 512 * j:512 * (j + 1)])
                        n1 = ve.inc(v.bn_stats(stats[:, 42:48], xt[:, 3584:4096]))
                        v.wait_ge(ve.h, n1)
                        n2 = ve.inc(v.bn_aggr(mv[:], stats[:]))
                        v.wait_ge(ve.h, n2)
                        ve.inc(v.tensor_scalar_add(veps[:], mv[:, 1:2], 1e-5))
                        v.wait_ge(ac.h, ac0 + rb + 1)
                        n4 = ve.inc(v.reciprocal(rstd[:], stdv[:]))
                        v.wait_ge(ve.h, n4)
                        ve.inc(v.tensor_scalar(xn_t[rb % 2][:], xt[:], mv[:, 0:1], rstd[:],
                                               op0=ALU.subtract, op1=ALU.mult))

                @blk.scalar
                def _(sc):
                    for rb in range(RB):
                        sc.wait_ge(ve.h, ve0 + PPR * rb + 3)
                        ac.inc(sc.activation(out=stdv[:], in_=veps[:], func=AF.Sqrt))

        with ExitStack() as c2:
            xn_fm = sb("xn_fm", [128, 32 * rows], BF16, c2)
            p_sb = sb("p_sb", [128, 17 * rows], BF16, c2)

            ld0 = s_ld.n
            with nc.Block() as blk:
                @blk.sync
                def _(sy):
                    for tc in range(32):
                        sy.dma_start_transpose(
                            out=xn_fm[:, tc * rows:(tc + 1) * rows],
                            in_=xnd_d[0:rows, tc * 128:(tc + 1) * 128]).then_inc(s_ld.h, 16)
                    sy.wait_ge(s_ld.h, ld0 + 16 * 32)
                s_ld.n += 16 * 32

            with ExitStack() as c3:
                d_buf = [sb(f"d_buf{i}", [128, 4096], BF16, c3) for i in range(2)]
                sqa = sb("sqa", [128, rows], F32, c3)
                sqb = sb("sqb", [128, rows], F32, c3)
                psF = [ps(f"psF{i}", [128, PSW], c3) for i in range(2)]

                fc_order = []
                for c in range(16):
                    fc_order += [c, 16 + c]

                ld0, mm0, ve0, ac0 = s_ld.n, mm.n, ve.n, ac.n
                with nc.Block() as blk:
                    @blk.sync
                    def _(sy):
                        for i, fc in enumerate(fc_order):
                            if i >= 2:
                                sy.wait_ge(mm.h, mm0 + i - 1)
                            dl.inc(i, sy.dma_start(out=d_buf[i % 2][:], in_=dt_d[fc]))

                    @blk.tensor
                    def _(t):
                        for i, fc in enumerate(fc_order):
                            t.wait_ge(dl.h(i), 16 * (i // 2 + 1))
                            if i >= 2:
                                t.wait_ge(ac.h, ac0 + i - 1)
                            pst = psF[i % 2]
                            last = None
                            for tc in range(32):
                                for (ro, rsz) in RH:
                                    last = t.matmul(
                                        pst[:, ro:ro + rsz],
                                        d_buf[i % 2][:, tc * 128:(tc + 1) * 128],
                                        xn_fm[:, tc * rows + ro:tc * rows + ro + rsz],
                                        start=(tc == 0), stop=(tc == 31))
                            mm.inc(last)

                    @blk.scalar
                    def _(sc):
                        for j in range(16):
                            sc.wait_ge(mm.h, mm0 + 2 * j + 2)
                            if j >= 1:
                                sc.wait_ge(ve.h, ve0 + j + 2)
                            ac.inc(sc.activation(out=sqa[:], in_=psF[0][:, 0:rows], func=AF.Square))
                            ac.inc(sc.activation(out=sqb[:], in_=psF[1][:, 0:rows], func=AF.Square))

                    @blk.vector
                    def _(v):
                        for j in range(16):
                            c = j
                            v.wait_ge(ac.h, ac0 + 2 * j + 2)
                            if c == 0:
                                n1 = ve.inc(v.tensor_tensor(p_sb[:, 0:rows], sqa[:], sqb[:], ALU.add))
                                ve.inc(v.tensor_copy(p_sb[:, 16 * rows:17 * rows], sqb[:]))
                                v.wait_ge(ve.h, n1)
                                ve.inc(v.tensor_copy(p_sb[0:1, 0:rows], sqa[0:1, :]))
                            else:
                                ve.inc(v.tensor_tensor(p_sb[:, c * rows:(c + 1) * rows],
                                                       sqa[:], sqb[:], ALU.add))

            with ExitStack() as c35:
                psP = [ps(f"psP{i}", [128, PSW], c35) for i in range(2)]
                u_buf = [sb(f"u_buf{i}", [128, 4096], BF16, c35) for i in range(2)]
                gld0, mm0, ve0 = g_ld.n, mm.n, ve.n
                with nc.Block() as blk:
                    @blk.gpsimd
                    def _(g):
                        for rc in range(8):
                            if rc >= 2:
                                g.wait_ge(mm.h, mm0 + rc - 1)
                            ul.inc(rc, g.dma_start(out=u_buf[rc % 2][:], in_=u1_d[rc]))

                    @blk.tensor
                    def _(t):
                        for rc in range(8):
                            t.wait_ge(ul.h(rc), 16 * (rc // 2 + 1))
                            if rc >= 2:
                                t.wait_ge(ve.h, ve0 + rc - 1)
                            last = None
                            for hc in range(32):
                                for (ro, rsz) in RH:
                                    last = t.matmul(
                                        psP[rc % 2][:, ro:ro + rsz],
                                        u_buf[rc % 2][:, hc * 128:(hc + 1) * 128],
                                        xn_fm[:, hc * rows + ro:hc * rows + ro + rsz],
                                        start=(hc == 0), stop=(hc == 31))
                            mm.inc(last)

                    @blk.vector
                    def _(v):
                        for rc in range(8):
                            v.wait_ge(mm.h, mm0 + rc + 1)
                            ve.inc(v.tensor_copy(P_sb[:, rc * rows:(rc + 1) * rows],
                                                 psP[rc % 2][:, 0:rows]))

            if debug:
                st0 = s_st.n
                with nc.Block() as blk:
                    @blk.sync
                    def _(sy):
                        sy.dma_start(out=pdump_d, in_=p_sb[:]).then_inc(s_st.h, 16)
                        sy.dma_start(out=pp_d, in_=P_sb[:]).then_inc(s_st.h, 16)
                        sy.wait_ge(s_st.h, st0 + 32)
                    s_st.n += 32

            with ExitStack() as c5:
                ci_buf = [sb(f"ci_buf{i}", [128, 17 * 512], BF16, c5) for i in range(2)]
                cev = sb("cev", [128, 4 * rows], BF16, c5)
                psC = [ps(f"psC{i}", [128, PSW], c5) for i in range(4)]
                cor_rr = cor_d.rearrange("(lg lc p) r -> lg p lc r", lg=8, lc=4, p=128)

                ld0, st0, mm0, ve0 = s_ld.n, s_st.n, mm.n, ve.n
                with nc.Block() as blk:
                    @blk.sync
                    def _(sy):
                        for lg in range(8):
                            if lg >= 2:
                                sy.wait_ge(mm.h, mm0 + lg - 1)
                            cl.inc(lg, sy.dma_start(out=ci_buf[lg % 2][:], in_=ci_d[lg]))
                            if lg >= 1:
                                sy.wait_ge(ve.h, ve0 + lg)
                                sy.dma_start(out=cor_rr[lg - 1],
                                             in_=cev[:].rearrange("p (lc r) -> p lc r", lc=4))\
                                  .then_inc(s_st.h, 16)
                        sy.wait_ge(ve.h, ve0 + 8)
                        sy.dma_start(out=cor_rr[7],
                                     in_=cev[:].rearrange("p (lc r) -> p lc r", lc=4))\
                          .then_inc(s_st.h, 16)
                        sy.wait_ge(s_st.h, st0 + 16 * 8)
                    s_st.n += 16 * 8

                    @blk.tensor
                    def _(t):
                        for lg in range(8):
                            t.wait_ge(cl.h(lg), 16 * (lg // 2 + 1))
                            if lg >= 1:
                                t.wait_ge(ve.h, ve0 + lg)
                            last = None
                            for cc in range(17):
                                for lc in range(4):
                                    for (ro, rsz) in RH:
                                        last = t.matmul(
                                            psC[lc][:, ro:ro + rsz],
                                            ci_buf[lg % 2][:, cc * 512 + lc * 128:cc * 512 + (lc + 1) * 128],
                                            p_sb[:, cc * rows + ro:cc * rows + ro + rsz],
                                            start=(cc == 0), stop=(cc == 16))
                            mm.inc(last)

                    @blk.vector
                    def _(v):
                        for lg in range(8):
                            v.wait_ge(mm.h, mm0 + lg + 1)
                            if lg >= 2:
                                v.wait_ge(s_st.h, st0 + 16 * lg)
                            for lc in range(3):
                                v.tensor_copy(cev[:, lc * rows:(lc + 1) * rows], psC[lc][:, 0:rows])
                            ve.inc(v.tensor_copy(cev[:, 3 * rows:4 * rows], psC[3][:, 0:rows]))

        with ExitStack() as c6:
            cor_t = [sb(f"cor_t{i}", [128, L], BF16, c6) for i in range(2)]
            rolled = [sb(f"rolled{i}", [128, L], BF16, c6) for i in range(2)]
            v8 = sb("v8", [128, 8], BF16, c6)
            d8 = sb("d8", [128, 8], U32, c6)
            idx = sb("idx", [128, 8], U32, c6)
            neg0 = sb("neg0", [128, 1], F32, c6)
            e8 = sb("e8", [128, 8], F32, c6)
            esum = sb("esum", [128, 1], F32, c6)
            erin = sb("erin", [128, 1], F32, c6)
            w8 = sb("w8", [128, 8], F32, c6)
            s_f = sb("s_f", [128, L], F32, c6)
            s_b = sb("s_bf", [128, L], BF16, c6)

            st0, ve0, ac0 = s_st.n, ve.n, ac.n
            VPR = 16
            SPR = 4 if debug else 1
            with nc.Block() as blk:
                @blk.sync
                def _(sy):
                    for rb in range(RB):
                        if rb == 0:
                            ct.inc(0, sy.dma_start_transpose(out=cor_t[0][:],
                                                             in_=cor_d[:, 0:128]))
                        if rb + 1 < RB:
                            if rb + 1 >= 2:
                                sy.wait_ge(ve.h, ve0 + VPR * (rb - 1) + 2)
                            ct.inc(rb + 1, sy.dma_start_transpose(
                                out=cor_t[(rb + 1) % 2][:],
                                in_=cor_d[:, (rb + 1) * 128:(rb + 2) * 128]))
                        sy.wait_ge(ve.h, ve0 + VPR * rb + 16)
                        sy.dma_start(out=s_d[rb * 128:(rb + 1) * 128, :],
                                     in_=s_b[:]).then_inc(s_st.h, 16)
                        if debug:
                            sy.dma_start(out=e8_d[rb * 128:(rb + 1) * 128, :], in_=e8[:])\
                              .then_inc(s_st.h, 16)
                            sy.dma_start(out=d8_d[rb * 128:(rb + 1) * 128, :], in_=d8[:])\
                              .then_inc(s_st.h, 16)
                            sy.dma_start(out=w8_d[rb * 128:(rb + 1) * 128, :], in_=w8[:])\
                              .then_inc(s_st.h, 16)
                    sy.wait_ge(s_st.h, st0 + 16 * SPR * RB)
                s_st.n += 16 * SPR * RB

                @blk.vector
                def _(v):
                    for rb in range(RB):
                        if rb >= 1:
                            v.wait_ge(s_st.h, st0 + 16 * SPR * rb)
                        v.wait_ge(ct.h(rb), 16 * (rb // 2 + 1))
                        n1 = ve.inc(v.max(out=v8[:], in_=cor_t[rb % 2][:]))
                        v.wait_ge(ve.h, n1)
                        n2 = ve.inc(v.max_index(out=d8[:], in_max=v8[:], in_values=cor_t[rb % 2][:]))
                        v.wait_ge(ve.h, n2)
                        ve.inc(v.tensor_tensor(idx[:], d8[:], riota[:].to_broadcast([128, 8]), ALU.add))
                        ve.inc(v.tensor_scalar_mul(neg0[:], v8[:, 0:1], -1.0))
                        v.wait_ge(ac.h, ac0 + rb + 1)
                        n5 = ve.inc(v.tensor_reduce(esum[:], e8[:], AX.X, ALU.add))
                        v.wait_ge(ve.h, n5)
                        n6 = ve.inc(v.reciprocal(erin[:], esum[:]))
                        v.wait_ge(ve.h, n6)
                        nw = ve.inc(v.tensor_scalar(w8[:], e8[:], erin[:], None, op0=ALU.mult))
                        v.wait_ge(ve.h, nw)
                        for k in range(8):
                            v.wait_ge(gs.h(k), 16 * ((8 * rb + k) // 2 + 1))
                            if k == 0:
                                nvk = ve.inc(v.tensor_scalar(s_f[:], rolled[0][:], w8[:, 0:1],
                                                             None, op0=ALU.mult))
                            else:
                                nvk = ve.inc(v.scalar_tensor_tensor(
                                    out=s_f[:], in0=rolled[k % 2][:], scalar=w8[:, k:k + 1],
                                    in1=s_f[:], op0=ALU.mult, op1=ALU.add))
                            v.wait_ge(ve.h, nvk)
                        ve.inc(v.tensor_copy(s_b[:], s_f[:]))

                @blk.scalar
                def _(sc):
                    for rb in range(RB):
                        sc.wait_ge(ve.h, ve0 + VPR * rb + 4)
                        ac.inc(sc.activation(out=e8[:], in_=v8[:], func=AF.Exp,
                                             bias=neg0[:], scale=1.0))

                @blk.gpsimd
                def _(g):
                    for rb in range(RB):
                        g.wait_ge(ve.h, ve0 + VPR * rb + 3)
                        for k in range(8):
                            if k >= 2:
                                g.wait_ge(ve.h, ve0 + VPR * rb + 8 + (k - 2))
                            elif rb >= 1:
                                g.wait_ge(ve.h, ve0 + VPR * (rb - 1) + 14 + k)
                            gs.inc(k, g.indirect_dma_start(
                                out=rolled[k % 2][:], out_offset=None,
                                in_=xnd_d,
                                in_offset=bass.IndirectOffsetOnAxis(idx[:, k:k + 1], 1),
                                element_offset=rb * 128 * 2 * L,
                            ))
            assert ve.n == ve0 + VPR * RB, (ve.n, ve0, RB)

        with ExitStack() as c7:
            Z_sb = sb("Z_sb", [128, 8 * rows], BF16, c7)

            with ExitStack() as c7a:
                act_sb = sb("act_sb", [128, NH_A * rows], BF16, c7a)
                v1g = [sb(f"v1g{i}", [128, 2048], BF16, c7a) for i in range(2)]
                v1u = [sb(f"v1u{i}", [128, 2048], BF16, c7a) for i in range(2)]
                u2b = [sb(f"u2b{i}", [128, 1024], BF16, c7a) for i in range(2)]
                silu_b = [sb(f"silu_b{i}", [128, rows], BF16, c7a) for i in range(2)]

                def v1_phase(half):
                    base_ii = 0 if half == 0 else NH_A
                    nii = NH_A if half == 0 else NH_B
                    nsteps = nii // 2
                    with ExitStack() as pctx:
                        psG = [ps(f"psG{half}_{i}", [128, PSW], pctx) for i in range(2)]
                        psU = [ps(f"psU{half}_{i}", [128, PSW], pctx) for i in range(2)]
                        mm0, ve0, ac0 = mm.n, ve.n, ac.n
                        vl_b = [vl.val(0), vl.val(1)]
                        with nc.Block() as blk:
                            @blk.gpsimd
                            def _(g):
                                for s in range(nsteps):
                                    if s >= 2:
                                        g.wait_ge(mm.h, mm0 + 2 * (2 * (s - 2) + 1) + 2)
                                    gg = base_ii // 2 + s
                                    vl.inc(s, g.dma_start(out=v1g[s % 2][:], in_=v1_d[gg]))
                                    vl.inc(s, g.dma_start(out=v1u[s % 2][:], in_=v1_d[43 + gg]))

                            @blk.tensor
                            def _(t):
                                for li in range(nii):
                                    s, i2 = li // 2, li % 2
                                    t.wait_ge(vl.h(s), vl_b[s % 2] + 32 * (s // 2 + 1))
                                    if li >= 2:
                                        t.wait_ge(ac.h, ac0 + (li - 2) + 1)
                                        t.wait_ge(ve.h, ve0 + (li - 2) + 1)
                                    last = None
                                    for rc in range(8):
                                        for (ro, rsz) in RH:
                                            last = t.matmul(
                                                psG[li % 2][:, ro:ro + rsz],
                                                v1g[s % 2][:, (rc * 2 + i2) * 128:(rc * 2 + i2 + 1) * 128],
                                                P_sb[:, rc * rows + ro:rc * rows + ro + rsz],
                                                start=(rc == 0), stop=(rc == 7))
                                    mm.inc(last)
                                    for rc in range(8):
                                        for (ro, rsz) in RH:
                                            last = t.matmul(
                                                psU[li % 2][:, ro:ro + rsz],
                                                v1u[s % 2][:, (rc * 2 + i2) * 128:(rc * 2 + i2 + 1) * 128],
                                                P_sb[:, rc * rows + ro:rc * rows + ro + rsz],
                                                start=(rc == 0), stop=(rc == 7))
                                    mm.inc(last)

                            @blk.scalar
                            def _(sc):
                                for li in range(nii):
                                    ii = base_ii + li
                                    sc.wait_ge(mm.h, mm0 + 2 * li + 1)
                                    if li >= 2:
                                        sc.wait_ge(ve.h, ve0 + (li - 2) + 1)
                                    ac.inc(sc.activation(out=silu_b[li % 2][:],
                                                         in_=psG[li % 2][:, 0:rows],
                                                         func=AF.Silu,
                                                         bias=b1_sb[:, ii:ii + 1], scale=1.0))

                            @blk.vector
                            def _(v):
                                for li in range(nii):
                                    ii = base_ii + li
                                    v.wait_ge(mm.h, mm0 + 2 * li + 2)
                                    v.wait_ge(ac.h, ac0 + li + 1)
                                    ve.inc(v.scalar_tensor_tensor(
                                        out=act_sb[:, li * rows:(li + 1) * rows],
                                        in0=psU[li % 2][:, 0:rows],
                                        scalar=b1_sb[:, 86 + ii:86 + ii + 1],
                                        in1=silu_b[li % 2][:], op0=ALU.add, op1=ALU.mult))

                def u2_phase(half):
                    base_ii = 0 if half == 0 else NH_A
                    nii = NH_A if half == 0 else NH_B
                    with ExitStack() as pctx:
                        psZ = [ps(f"psZ{half}_{i}", [128, min(512, rows)], pctx) for i in range(8)]
                        for (ro, rsz) in RH:
                            mm0, ve0 = mm.n, ve.n
                            us_b = [us.val(0), us.val(1)]
                            with nc.Block() as blk:
                                @blk.gpsimd
                                def _(g):
                                    for li in range(nii):
                                        if li >= 2:
                                            g.wait_ge(mm.h, mm0 + li - 1)
                                        us.inc(li, g.dma_start(out=u2b[li % 2][:],
                                                               in_=u2_d[base_ii + li]))

                                @blk.tensor
                                def _(t):
                                    for li in range(nii):
                                        t.wait_ge(us.h(li), us_b[li % 2] + 16 * (li // 2 + 1))
                                        last = None
                                        for rc in range(8):
                                            last = t.matmul(
                                                psZ[rc][:, 0:rsz],
                                                u2b[li % 2][:, rc * 128:(rc + 1) * 128],
                                                act_sb[:, li * rows + ro:li * rows + ro + rsz],
                                                start=(li == 0), stop=(li == nii - 1))
                                        mm.inc(last)

                                @blk.vector
                                def _(v):
                                    v.wait_ge(mm.h, mm0 + nii)
                                    for rc in range(7):
                                        zsl = Z_sb[:, rc * rows + ro:rc * rows + ro + rsz]
                                        if half == 0:
                                            v.tensor_copy(zsl, psZ[rc][:, 0:rsz])
                                        else:
                                            v.tensor_tensor(zsl, zsl, psZ[rc][:, 0:rsz], ALU.add)
                                    zsl = Z_sb[:, 7 * rows + ro:7 * rows + ro + rsz]
                                    if half == 0:
                                        ve.inc(v.tensor_copy(zsl, psZ[7][:, 0:rsz]))
                                    else:
                                        ve.inc(v.tensor_tensor(zsl, zsl, psZ[7][:, 0:rsz], ALU.add))

                v1_phase(0)
                u2_phase(0)
                v1_phase(1)
                u2_phase(1)

            with ExitStack() as c9:
                v2_sb = sb("v2_sb", [128, 8 * L], BF16, c9)
                b2_sb = sb("b2_sb", [128, L], BF16, c9)
                x_t2 = [sb(f"x_t2{i}", [128, L], F32, c9) for i in range(2)]
                s_t2 = sb("s_t2", [128, L], BF16, c9)
                f1 = sb("f1", [128, L], F32, c9)
                g_t = sb("g_t", [128, L], BF16, c9)
                o_t = [sb(f"o_t{i}", [128, L], F32, c9) for i in range(2)]
                psO = ps("psO", [128, L], c9)

                ld0, gld0, st0, mm0, ve0, ac0 = s_ld.n, g_ld.n, s_st.n, mm.n, ve.n, ac.n
                SPR = 2 if debug else 1
                with nc.Block() as blk:
                    @blk.gpsimd
                    def _(g):
                        g.dma_start(out=v2_sb[:], in_=v2_d).then_inc(g_ld.h, 16)
                        g.dma_start(out=b2_sb[:], in_=b2_d.to_broadcast([128, L])).then_inc(g_ld.h, 16)
                    g_ld.n += 32

                    @blk.sync
                    def _(sy):
                        for rb in range(RB):
                            if rb >= 2:
                                sy.wait_ge(ve.h, ve0 + 3 * (rb - 2) + 3)
                            x9.inc(rb, sy.dma_start(out=x_t2[rb % 2][:],
                                                    in_=x_d[rb * 128:(rb + 1) * 128, :]))
                            if rb >= 1:
                                sy.wait_ge(ve.h, ve0 + 3 * (rb - 1) + 2)
                            s9.inc(sy.dma_start(out=s_t2[:],
                                                in_=s_d[rb * 128:(rb + 1) * 128, :]), 16)
                            sy.wait_ge(ve.h, ve0 + 3 * rb + 3)
                            so.inc(rb, sy.dma_start(out=out_d[rb * 128:(rb + 1) * 128, :],
                                                    in_=o_t[rb % 2][:]))
                            if debug:
                                sy.dma_start(out=g_dump[rb * 128:(rb + 1) * 128, :],
                                             in_=g_t[:]).then_inc(s_st.h, 16)
                        if debug:
                            sy.dma_start(out=zdump_d, in_=Z_sb[:]).then_inc(s_st.h, 16)
                            sy.wait_ge(s_st.h, st0 + 16 * (RB + 1))
                        if (RB + 1) // 2 > 0:
                            sy.wait_ge(so.h(0), 16 * ((RB + 1) // 2))
                        if RB // 2 > 0:
                            sy.wait_ge(so.h(1), 16 * (RB // 2))

                    s_st.n += 16 * (RB + 1) if debug else 0

                    @blk.vector
                    def _(v):
                        for rb in range(RB):
                            v.wait_ge(mm.h, mm0 + rb + 1)
                            v.wait_ge(x9.h(rb), 16 * (rb // 2 + 1))
                            v.wait_ge(s9.h, 16 * (rb + 1))
                            v.wait_ge(g_ld.h, gld0 + 32)
                            ve.inc(v.tensor_tensor(f1[:], psO[:], b2_sb[:], ALU.add))
                            v.wait_ge(ac.h, ac0 + rb + 1)
                            nv9 = ve.inc(v.tensor_tensor(g_t[:], s_t2[:], g_t[:], ALU.mult))
                            v.wait_ge(ve.h, nv9)
                            if rb >= 2:
                                v.wait_ge(so.h(rb), 16 * (rb // 2))
                            ve.inc(v.tensor_tensor(o_t[rb % 2][:], x_t2[rb % 2][:], g_t[:], ALU.add))

                    @blk.scalar
                    def _(sc):
                        for rb in range(RB):
                            sc.wait_ge(ve.h, ve0 + 3 * rb + 1)
                            ac.inc(sc.activation(out=g_t[:], in_=f1[:], func=AF.Silu))

                    @blk.tensor
                    def _(t):
                        t.wait_ge(g_ld.h, gld0 + 16)
                        for rb in range(RB):
                            if rb >= 1:
                                t.wait_ge(ve.h, ve0 + 3 * (rb - 1) + 1)
                            last = None
                            for rc in range(8):
                                for hb in range(8):
                                    last = t.matmul(
                                        psO[:, hb * 512:(hb + 1) * 512],
                                        Z_sb[:, rc * rows + rb * 128:rc * rows + (rb + 1) * 128],
                                        v2_sb[:, rc * L + hb * 512:rc * L + (hb + 1) * 512],
                                        start=(rc == 0), stop=(rc == 7))
                            mm.inc(last)

    return nc


def _host_layouts(U1, V1, U2, V2, b1):
    U1t = np.ascontiguousarray(
        np.asarray(U1, np.float32).reshape(32, 128, 8, 128).transpose(2, 1, 0, 3)
    ).reshape(8, 128, 4096)
    arr = np.asarray(V1, np.float32).reshape(8, 128, 172, 128).transpose(2, 1, 0, 3)
    V1t = np.ascontiguousarray(
        arr.reshape(86, 2, 128, 8, 128).transpose(0, 2, 3, 1, 4)
    ).reshape(86, 128, 2048)
    U2t = np.ascontiguousarray(
        np.asarray(U2, np.float32).reshape(86, 128, 8, 128)
    ).reshape(86, 128, 1024)
    V2t = np.ascontiguousarray(
        np.asarray(V2, np.float32).reshape(8, 128, 4096).transpose(1, 0, 2)
    ).reshape(128, 8 * 4096)
    b1t = np.ascontiguousarray(np.asarray(b1, np.float32).reshape(172, 128).T)
    return U1t, V1t, U2t, V2t, b1t


def kernel(L0_attn_x, L0_ffn_U1, L0_ffn_V1, L0_ffn_U2, L0_ffn_V2, L0_ffn_b1, L0_ffn_b2):
    x = np.ascontiguousarray(np.asarray(L0_attn_x, np.float32)).reshape(8192, 4096)
    U1t, V1t, U2t, V2t, b1t = _host_layouts(L0_ffn_U1, L0_ffn_V1, L0_ffn_U2,
                                            L0_ffn_V2, L0_ffn_b1)
    b2 = np.ascontiguousarray(np.asarray(L0_ffn_b2, np.float32)).reshape(1, 4096)
    Dt, Cit = dft_consts()

    rows = 8192 // N_CORES
    nc = build_program(rows, debug=False)
    in_maps = []
    for c in range(N_CORES):
        in_maps.append({
            "x": np.ascontiguousarray(x[c * rows:(c + 1) * rows]),
            "U1t": U1t, "V1t": V1t, "U2t": U2t, "V2t": V2t,
            "b1t": b1t, "b2": b2, "Dt": Dt, "Cit": Cit,
        })
    res = run_bass_kernel_spmd(nc, in_maps, core_ids=list(range(N_CORES)))
    out = np.concatenate([res.results[c]["out"] for c in range(N_CORES)], axis=0)
    return out.reshape(2, 4096, 4096)


# version 05
# speedup vs baseline: 1.0072x; 1.0072x over previous
import numpy as np
import ml_dtypes
from contextlib import ExitStack

import concourse.bass as bass
import concourse.mybir as mybir
from concourse.bass_utils import run_bass_kernel_spmd

F32 = mybir.dt.float32
BF16 = mybir.dt.bfloat16
U32 = mybir.dt.uint32
AF = mybir.ActivationFunctionType
ALU = mybir.AluOpType
AX = mybir.AxisListType
BF = ml_dtypes.bfloat16

L = 4096
NF = 2049
NII = 86
NH_A = 44
NH_B = NII - NH_A
N_CORES = 8


class Sem:
    def __init__(self, ctx, nc, name):
        self.h = ctx.enter_context(nc.semaphore(name))
        self.n = 0

    def inc(self, inst, k=1):
        inst.then_inc(self.h, k)
        self.n += k
        return self.n


class SemPair:

    def __init__(self, ctx, nc, name, n=2):
        self.s = [Sem(ctx, nc, f"{name}{i}") for i in range(n)]

    def inc(self, i, inst, k=16):
        return self.s[i % len(self.s)].inc(inst, k)

    def h(self, i):
        return self.s[i % len(self.s)].h

    def val(self, i):
        return self.s[i % len(self.s)].n


def _row_halves(rows):
    out, o = [], 0
    while o < rows:
        sz = min(512, rows - o)
        out.append((o, sz))
        o += sz
    return out


_CONST_CACHE = {}


def dft_consts():
    if "D" in _CONST_CACHE:
        return _CONST_CACHE["D"], _CONST_CACHE["Ci"]
    t = np.arange(L, dtype=np.float64)
    f = np.arange(NF, dtype=np.float64)
    ang = 2.0 * np.pi * np.outer(t, f) / L
    D = np.empty((L, L), dtype=np.float64)
    D[:, 0:NF] = np.cos(ang)
    D[:, NF:] = np.sin(ang[:, 1:2048])
    D_t = np.ascontiguousarray(
        D.astype(BF).reshape(32, 128, 32, 128).transpose(2, 1, 0, 3)
    ).reshape(32, 128, 4096)

    lf = np.arange(2176, dtype=np.float64)
    a = np.zeros(2176, dtype=np.float64)
    a[0] = 1.0 / L
    a[2048] = 1.0 / L
    a[1:2048] = 2.0 / L
    li = np.arange(L, dtype=np.float64)
    Ci = (a[:, None] * np.cos(2.0 * np.pi * np.outer(lf, li) / L)).astype(BF)
    Ci[2049:, :] = 0
    Ci_t = np.ascontiguousarray(
        Ci.reshape(17, 128, 8, 512).transpose(2, 1, 0, 3)
    ).reshape(8, 128, 17 * 512)
    _CONST_CACHE["D"] = D_t
    _CONST_CACHE["Ci"] = Ci_t
    return D_t, Ci_t


def build_program(rows, debug=False):
    RB = rows // 128
    RH = _row_halves(rows)
    PSW = rows if rows <= 512 else 1024

    nc = bass.Bass("TRN2", target_bir_lowering=False)

    x_d = nc.dram_tensor("x", [rows, L], F32, kind="ExternalInput").ap()
    u1_d = nc.dram_tensor("U1t", [8, 128, 4096], F32, kind="ExternalInput").ap()
    v1_d = nc.dram_tensor("V1t", [NII, 128, 2048], F32, kind="ExternalInput").ap()
    u2_d = nc.dram_tensor("U2t", [NII, 128, 1024], F32, kind="ExternalInput").ap()
    v2_d = nc.dram_tensor("V2t", [128, 8 * L], F32, kind="ExternalInput").ap()
    b1_d = nc.dram_tensor("b1t", [128, 172], F32, kind="ExternalInput").ap()
    b2_d = nc.dram_tensor("b2", [1, L], F32, kind="ExternalInput").ap()
    dt_d = nc.dram_tensor("Dt", [32, 128, 4096], BF16, kind="ExternalInput").ap()
    ci_d = nc.dram_tensor("Cit", [8, 128, 17 * 512], BF16, kind="ExternalInput").ap()

    out_d = nc.dram_tensor("out", [rows, L], F32, kind="ExternalOutput").ap()

    skind = "ExternalOutput" if debug else "Internal"
    xnd_d = nc.dram_tensor("xnd", [rows, 2 * L], BF16, kind=skind).ap()
    cor_d = nc.dram_tensor("corrd", [L, rows], BF16, kind=skind).ap()
    s_d = nc.dram_tensor("sdram", [rows, L], BF16, kind=skind).ap()
    if debug:
        pdump_d = nc.dram_tensor("pdump", [128, 17 * rows], BF16, kind="ExternalOutput").ap()
        zdump_d = nc.dram_tensor("zdump", [128, 8 * rows], BF16, kind="ExternalOutput").ap()
        e8_d = nc.dram_tensor("e8d", [rows, 8], F32, kind="ExternalOutput").ap()
        d8_d = nc.dram_tensor("d8d", [rows, 8], U32, kind="ExternalOutput").ap()
        w8_d = nc.dram_tensor("w8d", [rows, 8], F32, kind="ExternalOutput").ap()
        pp_d = nc.dram_tensor("ppd", [128, 8 * rows], BF16, kind="ExternalOutput").ap()
        g_dump = nc.dram_tensor("gdump", [rows, L], BF16, kind="ExternalOutput").ap()

    with ExitStack() as CTX:
        def sb(name, shape, dtype, ctx=None):
            return (ctx or CTX).enter_context(nc.sbuf_tensor(name, shape, dtype))

        def ps(name, shape, ctx=None):
            return (ctx or CTX).enter_context(nc.psum_tensor(name, shape, F32))

        s_ld = Sem(CTX, nc, "s_ld")
        g_ld = Sem(CTX, nc, "g_ld")
        s_st = Sem(CTX, nc, "s_st")
        mm = Sem(CTX, nc, "mm")
        lx = SemPair(CTX, nc, "lx")
        dl = SemPair(CTX, nc, "dl")
        ul = SemPair(CTX, nc, "ul")
        cl = SemPair(CTX, nc, "cl")
        ct = SemPair(CTX, nc, "ct")
        gs = SemPair(CTX, nc, "gs")
        vl = SemPair(CTX, nc, "vl")
        us = SemPair(CTX, nc, "us")
        x9 = SemPair(CTX, nc, "x9")
        s9 = Sem(CTX, nc, "s9")
        so = SemPair(CTX, nc, "so")
        ve = Sem(CTX, nc, "ve")
        ac = Sem(CTX, nc, "ac")
        gp = Sem(CTX, nc, "gp")

        P_sb = sb("P_sb", [128, 8 * rows], BF16)
        b1_sb = sb("b1_sb", [128, 172], F32)
        riota = sb("riota", [128, 1], U32)

        with nc.Block() as blk:
            @blk.sync
            def _(sy):
                s_ld.inc(sy.dma_start(out=b1_sb[:], in_=b1_d), 16)

            @blk.gpsimd
            def _(g):
                gp.inc(g.iota(riota[:], [[0, 1]], base=0, channel_multiplier=2 * L))

        with ExitStack() as c1:
            x_t = [sb(f"x_t{i}", [128, L], F32, c1) for i in range(2)]
            xn_t = [sb(f"xn_t{i}", [128, L], BF16, c1) for i in range(2)]
            stats = sb("stats", [128, 48], F32, c1)
            mv = sb("mv", [128, 2], F32, c1)
            veps = sb("veps", [128, 1], F32, c1)
            stdv = sb("stdv", [128, 1], F32, c1)
            rstd = sb("rstd", [128, 1], F32, c1)

            ld0, st0, ve0, ac0 = s_ld.n, s_st.n, ve.n, ac.n
            PPR = 5
            with nc.Block() as blk:
                @blk.sync
                def _(sy):
                    for rb in range(RB):
                        if rb >= 2:
                            sy.wait_ge(ve.h, ve0 + PPR * (rb - 2) + 5)
                            sy.wait_ge(s_st.h, st0 + 32 * rb)
                        lx.inc(rb, sy.dma_start(out=x_t[rb % 2][:],
                                                in_=x_d[rb * 128:(rb + 1) * 128, :]))
                        sy.wait_ge(ve.h, ve0 + PPR * rb + 5)
                        sy.dma_start(out=xnd_d[rb * 128:(rb + 1) * 128, 0:L],
                                     in_=xn_t[rb % 2][:]).then_inc(s_st.h, 16)
                        sy.dma_start(out=xnd_d[rb * 128:(rb + 1) * 128, L:2 * L],
                                     in_=xn_t[rb % 2][:]).then_inc(s_st.h, 16)
                    sy.wait_ge(s_st.h, st0 + 32 * RB)
                s_st.n += 32 * RB

                @blk.vector
                def _(v):
                    for rb in range(RB):
                        v.wait_ge(s_ld.h, ld0)
                        v.wait_ge(lx.h(rb), 16 * (rb // 2 + 1))
                        xt = x_t[rb % 2]
                        for j in range(7):
                            v.bn_stats(stats[:, 6 * j:6 * (j + 1)], xt[:, 512 * j:512 * (j + 1)])
                        n1 = ve.inc(v.bn_stats(stats[:, 42:48], xt[:, 3584:4096]))
                        v.wait_ge(ve.h, n1)
                        n2 = ve.inc(v.bn_aggr(mv[:], stats[:]))
                        v.wait_ge(ve.h, n2)
                        ve.inc(v.tensor_scalar_add(veps[:], mv[:, 1:2], 1e-5))
                        v.wait_ge(ac.h, ac0 + rb + 1)
                        n4 = ve.inc(v.reciprocal(rstd[:], stdv[:]))
                        v.wait_ge(ve.h, n4)
                        ve.inc(v.tensor_scalar(xn_t[rb % 2][:], xt[:], mv[:, 0:1], rstd[:],
                                               op0=ALU.subtract, op1=ALU.mult))

                @blk.scalar
                def _(sc):
                    for rb in range(RB):
                        sc.wait_ge(ve.h, ve0 + PPR * rb + 3)
                        ac.inc(sc.activation(out=stdv[:], in_=veps[:], func=AF.Sqrt))

        with ExitStack() as c2:
            xn_fm = sb("xn_fm", [128, 32 * rows], BF16, c2)
            p_sb = sb("p_sb", [128, 17 * rows], BF16, c2)

            ld0 = s_ld.n
            with nc.Block() as blk:
                @blk.sync
                def _(sy):
                    for tc in range(32):
                        sy.dma_start_transpose(
                            out=xn_fm[:, tc * rows:(tc + 1) * rows],
                            in_=xnd_d[0:rows, tc * 128:(tc + 1) * 128]).then_inc(s_ld.h, 16)
                    sy.wait_ge(s_ld.h, ld0 + 16 * 32)
                s_ld.n += 16 * 32

            with ExitStack() as c3:
                d_buf = [sb(f"d_buf{i}", [128, 4096], BF16, c3) for i in range(2)]
                sqa = sb("sqa", [128, rows], F32, c3)
                sqb = sb("sqb", [128, rows], F32, c3)
                psF = [ps(f"psF{i}", [128, PSW], c3) for i in range(2)]

                fc_order = []
                for c in range(16):
                    fc_order += [c, 16 + c]

                ld0, mm0, ve0, ac0 = s_ld.n, mm.n, ve.n, ac.n
                with nc.Block() as blk:
                    @blk.sync
                    def _(sy):
                        for i, fc in enumerate(fc_order):
                            if i >= 2:
                                sy.wait_ge(mm.h, mm0 + i - 1)
                            dl.inc(i, sy.dma_start(out=d_buf[i % 2][:], in_=dt_d[fc]))

                    @blk.tensor
                    def _(t):
                        for i, fc in enumerate(fc_order):
                            t.wait_ge(dl.h(i), 16 * (i // 2 + 1))
                            if i >= 2:
                                t.wait_ge(ac.h, ac0 + i - 1)
                            pst = psF[i % 2]
                            last = None
                            for tc in range(32):
                                for (ro, rsz) in RH:
                                    last = t.matmul(
                                        pst[:, ro:ro + rsz],
                                        d_buf[i % 2][:, tc * 128:(tc + 1) * 128],
                                        xn_fm[:, tc * rows + ro:tc * rows + ro + rsz],
                                        start=(tc == 0), stop=(tc == 31))
                            mm.inc(last)

                    @blk.scalar
                    def _(sc):
                        for j in range(16):
                            sc.wait_ge(mm.h, mm0 + 2 * j + 2)
                            if j >= 1:
                                sc.wait_ge(ve.h, ve0 + j + 2)
                            ac.inc(sc.activation(out=sqa[:], in_=psF[0][:, 0:rows], func=AF.Square))
                            ac.inc(sc.activation(out=sqb[:], in_=psF[1][:, 0:rows], func=AF.Square))

                    @blk.vector
                    def _(v):
                        for j in range(16):
                            c = j
                            v.wait_ge(ac.h, ac0 + 2 * j + 2)
                            if c == 0:
                                n1 = ve.inc(v.tensor_tensor(p_sb[:, 0:rows], sqa[:], sqb[:], ALU.add))
                                ve.inc(v.tensor_copy(p_sb[:, 16 * rows:17 * rows], sqb[:]))
                                v.wait_ge(ve.h, n1)
                                ve.inc(v.tensor_copy(p_sb[0:1, 0:rows], sqa[0:1, :]))
                            else:
                                ve.inc(v.tensor_tensor(p_sb[:, c * rows:(c + 1) * rows],
                                                       sqa[:], sqb[:], ALU.add))

            with ExitStack() as c35:
                psP = [ps(f"psP{i}", [128, PSW], c35) for i in range(2)]
                u_buf = [sb(f"u_buf{i}", [128, 4096], BF16, c35) for i in range(2)]
                gld0, mm0, ve0 = g_ld.n, mm.n, ve.n
                with nc.Block() as blk:
                    @blk.gpsimd
                    def _(g):
                        for rc in range(8):
                            if rc >= 2:
                                g.wait_ge(mm.h, mm0 + rc - 1)
                            ul.inc(rc, g.dma_start(out=u_buf[rc % 2][:], in_=u1_d[rc]))

                    @blk.tensor
                    def _(t):
                        for rc in range(8):
                            t.wait_ge(ul.h(rc), 16 * (rc // 2 + 1))
                            if rc >= 2:
                                t.wait_ge(ve.h, ve0 + rc - 1)
                            last = None
                            for hc in range(32):
                                for (ro, rsz) in RH:
                                    last = t.matmul(
                                        psP[rc % 2][:, ro:ro + rsz],
                                        u_buf[rc % 2][:, hc * 128:(hc + 1) * 128],
                                        xn_fm[:, hc * rows + ro:hc * rows + ro + rsz],
                                        start=(hc == 0), stop=(hc == 31))
                            mm.inc(last)

                    @blk.vector
                    def _(v):
                        for rc in range(8):
                            v.wait_ge(mm.h, mm0 + rc + 1)
                            ve.inc(v.tensor_copy(P_sb[:, rc * rows:(rc + 1) * rows],
                                                 psP[rc % 2][:, 0:rows]))

            if debug:
                st0 = s_st.n
                with nc.Block() as blk:
                    @blk.sync
                    def _(sy):
                        sy.dma_start(out=pdump_d, in_=p_sb[:]).then_inc(s_st.h, 16)
                        sy.dma_start(out=pp_d, in_=P_sb[:]).then_inc(s_st.h, 16)
                        sy.wait_ge(s_st.h, st0 + 32)
                    s_st.n += 32

            with ExitStack() as c5:
                ci_buf = [sb(f"ci_buf{i}", [128, 17 * 512], BF16, c5) for i in range(2)]
                cev = sb("cev", [128, 4 * rows], BF16, c5)
                psC = [ps(f"psC{i}", [128, PSW], c5) for i in range(4)]
                cor_rr = cor_d.rearrange("(lg lc p) r -> lg p lc r", lg=8, lc=4, p=128)

                ld0, st0, mm0, ve0 = s_ld.n, s_st.n, mm.n, ve.n
                with nc.Block() as blk:
                    @blk.sync
                    def _(sy):
                        for lg in range(8):
                            if lg >= 2:
                                sy.wait_ge(mm.h, mm0 + lg - 1)
                            cl.inc(lg, sy.dma_start(out=ci_buf[lg % 2][:], in_=ci_d[lg]))
                            if lg >= 1:
                                sy.wait_ge(ve.h, ve0 + lg)
                                sy.dma_start(out=cor_rr[lg - 1],
                                             in_=cev[:].rearrange("p (lc r) -> p lc r", lc=4))\
                                  .then_inc(s_st.h, 16)
                        sy.wait_ge(ve.h, ve0 + 8)
                        sy.dma_start(out=cor_rr[7],
                                     in_=cev[:].rearrange("p (lc r) -> p lc r", lc=4))\
                          .then_inc(s_st.h, 16)
                        sy.wait_ge(s_st.h, st0 + 16 * 8)
                    s_st.n += 16 * 8

                    @blk.tensor
                    def _(t):
                        for lg in range(8):
                            t.wait_ge(cl.h(lg), 16 * (lg // 2 + 1))
                            if lg >= 1:
                                t.wait_ge(ve.h, ve0 + lg)
                            last = None
                            for cc in range(17):
                                for lc in range(4):
                                    for (ro, rsz) in RH:
                                        last = t.matmul(
                                            psC[lc][:, ro:ro + rsz],
                                            ci_buf[lg % 2][:, cc * 512 + lc * 128:cc * 512 + (lc + 1) * 128],
                                            p_sb[:, cc * rows + ro:cc * rows + ro + rsz],
                                            start=(cc == 0), stop=(cc == 16))
                            mm.inc(last)

                    @blk.vector
                    def _(v):
                        for lg in range(8):
                            v.wait_ge(mm.h, mm0 + lg + 1)
                            if lg >= 1:
                                v.wait_ge(s_st.h, st0 + 16 * lg)
                            for lc in range(3):
                                v.tensor_copy(cev[:, lc * rows:(lc + 1) * rows], psC[lc][:, 0:rows])
                            ve.inc(v.tensor_copy(cev[:, 3 * rows:4 * rows], psC[3][:, 0:rows]))

        with ExitStack() as c6:
            cor_t = [sb(f"cor_t{i}", [128, L], BF16, c6) for i in range(2)]
            rolled = [sb(f"rolled{i}", [128, L], BF16, c6) for i in range(2)]
            v8 = sb("v8", [128, 8], BF16, c6)
            d8 = sb("d8", [128, 8], U32, c6)
            idx = sb("idx", [128, 8], U32, c6)
            neg0 = sb("neg0", [128, 1], F32, c6)
            e8 = sb("e8", [128, 8], F32, c6)
            esum = sb("esum", [128, 1], F32, c6)
            erin = sb("erin", [128, 1], F32, c6)
            w8 = sb("w8", [128, 8], F32, c6)
            s_f = sb("s_f", [128, L], F32, c6)
            s_b = sb("s_bf", [128, L], BF16, c6)

            st0, ve0, ac0 = s_st.n, ve.n, ac.n
            VPR = 16
            SPR = 4 if debug else 1
            with nc.Block() as blk:
                @blk.sync
                def _(sy):
                    for rb in range(RB):
                        if rb == 0:
                            ct.inc(0, sy.dma_start_transpose(out=cor_t[0][:],
                                                             in_=cor_d[:, 0:128]))
                        if rb + 1 < RB:
                            if rb + 1 >= 2:
                                sy.wait_ge(ve.h, ve0 + VPR * (rb - 1) + 2)
                            ct.inc(rb + 1, sy.dma_start_transpose(
                                out=cor_t[(rb + 1) % 2][:],
                                in_=cor_d[:, (rb + 1) * 128:(rb + 2) * 128]))
                        sy.wait_ge(ve.h, ve0 + VPR * rb + 16)
                        sy.dma_start(out=s_d[rb * 128:(rb + 1) * 128, :],
                                     in_=s_b[:]).then_inc(s_st.h, 16)
                        if debug:
                            sy.dma_start(out=e8_d[rb * 128:(rb + 1) * 128, :], in_=e8[:])\
                              .then_inc(s_st.h, 16)
                            sy.dma_start(out=d8_d[rb * 128:(rb + 1) * 128, :], in_=d8[:])\
                              .then_inc(s_st.h, 16)
                            sy.dma_start(out=w8_d[rb * 128:(rb + 1) * 128, :], in_=w8[:])\
                              .then_inc(s_st.h, 16)
                    sy.wait_ge(s_st.h, st0 + 16 * SPR * RB)
                s_st.n += 16 * SPR * RB

                @blk.vector
                def _(v):
                    for rb in range(RB):
                        if rb >= 1:
                            v.wait_ge(s_st.h, st0 + 16 * SPR * rb)
                        v.wait_ge(ct.h(rb), 16 * (rb // 2 + 1))
                        n1 = ve.inc(v.max(out=v8[:], in_=cor_t[rb % 2][:]))
                        v.wait_ge(ve.h, n1)
                        n2 = ve.inc(v.max_index(out=d8[:], in_max=v8[:], in_values=cor_t[rb % 2][:]))
                        v.wait_ge(ve.h, n2)
                        ve.inc(v.tensor_tensor(idx[:], d8[:], riota[:].to_broadcast([128, 8]), ALU.add))
                        ve.inc(v.tensor_scalar_mul(neg0[:], v8[:, 0:1], -1.0))
                        v.wait_ge(ac.h, ac0 + rb + 1)
                        n5 = ve.inc(v.tensor_reduce(esum[:], e8[:], AX.X, ALU.add))
                        v.wait_ge(ve.h, n5)
                        n6 = ve.inc(v.reciprocal(erin[:], esum[:]))
                        v.wait_ge(ve.h, n6)
                        nw = ve.inc(v.tensor_scalar(w8[:], e8[:], erin[:], None, op0=ALU.mult))
                        v.wait_ge(ve.h, nw)
                        for k in range(8):
                            v.wait_ge(gs.h(k), 16 * ((8 * rb + k) // 2 + 1))
                            if k == 0:
                                nvk = ve.inc(v.tensor_scalar(s_f[:], rolled[0][:], w8[:, 0:1],
                                                             None, op0=ALU.mult))
                            else:
                                nvk = ve.inc(v.scalar_tensor_tensor(
                                    out=s_f[:], in0=rolled[k % 2][:], scalar=w8[:, k:k + 1],
                                    in1=s_f[:], op0=ALU.mult, op1=ALU.add))
                            v.wait_ge(ve.h, nvk)
                        ve.inc(v.tensor_copy(s_b[:], s_f[:]))

                @blk.scalar
                def _(sc):
                    for rb in range(RB):
                        sc.wait_ge(ve.h, ve0 + VPR * rb + 4)
                        ac.inc(sc.activation(out=e8[:], in_=v8[:], func=AF.Exp,
                                             bias=neg0[:], scale=1.0))

                @blk.gpsimd
                def _(g):
                    for rb in range(RB):
                        g.wait_ge(ve.h, ve0 + VPR * rb + 3)
                        for k in range(8):
                            if k >= 2:
                                g.wait_ge(ve.h, ve0 + VPR * rb + 8 + (k - 2))
                            elif rb >= 1:
                                g.wait_ge(ve.h, ve0 + VPR * (rb - 1) + 14 + k)
                            gs.inc(k, g.indirect_dma_start(
                                out=rolled[k % 2][:], out_offset=None,
                                in_=xnd_d,
                                in_offset=bass.IndirectOffsetOnAxis(idx[:, k:k + 1], 1),
                                element_offset=rb * 128 * 2 * L,
                            ))
            assert ve.n == ve0 + VPR * RB, (ve.n, ve0, RB)

        with ExitStack() as c7:
            Z_sb = sb("Z_sb", [128, 8 * rows], BF16, c7)

            with ExitStack() as c7a:
                act_sb = sb("act_sb", [128, NH_A * rows], BF16, c7a)
                v1g = [sb(f"v1g{i}", [128, 2048], BF16, c7a) for i in range(2)]
                v1u = [sb(f"v1u{i}", [128, 2048], BF16, c7a) for i in range(2)]
                u2b = [sb(f"u2b{i}", [128, 1024], BF16, c7a) for i in range(2)]
                silu_b = [sb(f"silu_b{i}", [128, rows], BF16, c7a) for i in range(2)]

                def v1_phase(half):
                    base_ii = 0 if half == 0 else NH_A
                    nii = NH_A if half == 0 else NH_B
                    nsteps = nii // 2
                    with ExitStack() as pctx:
                        psG = [ps(f"psG{half}_{i}", [128, PSW], pctx) for i in range(2)]
                        psU = [ps(f"psU{half}_{i}", [128, PSW], pctx) for i in range(2)]
                        mm0, ve0, ac0 = mm.n, ve.n, ac.n
                        vl_b = [vl.val(0), vl.val(1)]
                        with nc.Block() as blk:
                            @blk.gpsimd
                            def _(g):
                                for s in range(nsteps):
                                    if s >= 2:
                                        g.wait_ge(mm.h, mm0 + 2 * (2 * (s - 2) + 1) + 2)
                                    gg = base_ii // 2 + s
                                    vl.inc(s, g.dma_start(out=v1g[s % 2][:], in_=v1_d[gg]))
                                    vl.inc(s, g.dma_start(out=v1u[s % 2][:], in_=v1_d[43 + gg]))

                            @blk.tensor
                            def _(t):
                                for li in range(nii):
                                    s, i2 = li // 2, li % 2
                                    t.wait_ge(vl.h(s), vl_b[s % 2] + 32 * (s // 2 + 1))
                                    if li >= 2:
                                        t.wait_ge(ac.h, ac0 + (li - 2) + 1)
                                        t.wait_ge(ve.h, ve0 + (li - 2) + 1)
                                    last = None
                                    for rc in range(8):
                                        for (ro, rsz) in RH:
                                            last = t.matmul(
                                                psG[li % 2][:, ro:ro + rsz],
                                                v1g[s % 2][:, (rc * 2 + i2) * 128:(rc * 2 + i2 + 1) * 128],
                                                P_sb[:, rc * rows + ro:rc * rows + ro + rsz],
                                                start=(rc == 0), stop=(rc == 7))
                                    mm.inc(last)
                                    for rc in range(8):
                                        for (ro, rsz) in RH:
                                            last = t.matmul(
                                                psU[li % 2][:, ro:ro + rsz],
                                                v1u[s % 2][:, (rc * 2 + i2) * 128:(rc * 2 + i2 + 1) * 128],
                                                P_sb[:, rc * rows + ro:rc * rows + ro + rsz],
                                                start=(rc == 0), stop=(rc == 7))
                                    mm.inc(last)

                            @blk.scalar
                            def _(sc):
                                for li in range(nii):
                                    ii = base_ii + li
                                    sc.wait_ge(mm.h, mm0 + 2 * li + 1)
                                    if li >= 2:
                                        sc.wait_ge(ve.h, ve0 + (li - 2) + 1)
                                    ac.inc(sc.activation(out=silu_b[li % 2][:],
                                                         in_=psG[li % 2][:, 0:rows],
                                                         func=AF.Silu,
                                                         bias=b1_sb[:, ii:ii + 1], scale=1.0))

                            @blk.vector
                            def _(v):
                                for li in range(nii):
                                    ii = base_ii + li
                                    v.wait_ge(mm.h, mm0 + 2 * li + 2)
                                    v.wait_ge(ac.h, ac0 + li + 1)
                                    ve.inc(v.scalar_tensor_tensor(
                                        out=act_sb[:, li * rows:(li + 1) * rows],
                                        in0=psU[li % 2][:, 0:rows],
                                        scalar=b1_sb[:, 86 + ii:86 + ii + 1],
                                        in1=silu_b[li % 2][:], op0=ALU.add, op1=ALU.mult))

                def u2_phase(half):
                    base_ii = 0 if half == 0 else NH_A
                    nii = NH_A if half == 0 else NH_B
                    with ExitStack() as pctx:
                        psZ = [ps(f"psZ{half}_{i}", [128, min(512, rows)], pctx) for i in range(8)]
                        for (ro, rsz) in RH:
                            mm0, ve0 = mm.n, ve.n
                            us_b = [us.val(0), us.val(1)]
                            with nc.Block() as blk:
                                @blk.gpsimd
                                def _(g):
                                    for li in range(nii):
                                        if li >= 2:
                                            g.wait_ge(mm.h, mm0 + li - 1)
                                        us.inc(li, g.dma_start(out=u2b[li % 2][:],
                                                               in_=u2_d[base_ii + li]))

                                @blk.tensor
                                def _(t):
                                    for li in range(nii):
                                        t.wait_ge(us.h(li), us_b[li % 2] + 16 * (li // 2 + 1))
                                        last = None
                                        for rc in range(8):
                                            last = t.matmul(
                                                psZ[rc][:, 0:rsz],
                                                u2b[li % 2][:, rc * 128:(rc + 1) * 128],
                                                act_sb[:, li * rows + ro:li * rows + ro + rsz],
                                                start=(li == 0), stop=(li == nii - 1))
                                        mm.inc(last)

                                @blk.vector
                                def _(v):
                                    v.wait_ge(mm.h, mm0 + nii)
                                    for rc in range(7):
                                        zsl = Z_sb[:, rc * rows + ro:rc * rows + ro + rsz]
                                        if half == 0:
                                            v.tensor_copy(zsl, psZ[rc][:, 0:rsz])
                                        else:
                                            v.tensor_tensor(zsl, zsl, psZ[rc][:, 0:rsz], ALU.add)
                                    zsl = Z_sb[:, 7 * rows + ro:7 * rows + ro + rsz]
                                    if half == 0:
                                        ve.inc(v.tensor_copy(zsl, psZ[7][:, 0:rsz]))
                                    else:
                                        ve.inc(v.tensor_tensor(zsl, zsl, psZ[7][:, 0:rsz], ALU.add))

                v1_phase(0)
                u2_phase(0)
                v1_phase(1)
                u2_phase(1)

            with ExitStack() as c9:
                v2_sb = sb("v2_sb", [128, 8 * L], BF16, c9)
                b2_sb = sb("b2_sb", [128, L], BF16, c9)
                x_t2 = [sb(f"x_t2{i}", [128, L], F32, c9) for i in range(2)]
                s_t2 = sb("s_t2", [128, L], BF16, c9)
                f1 = sb("f1", [128, L], F32, c9)
                g_t = sb("g_t", [128, L], BF16, c9)
                o_t = [sb(f"o_t{i}", [128, L], F32, c9) for i in range(2)]
                psO = ps("psO", [128, L], c9)

                ld0, gld0, st0, mm0, ve0, ac0 = s_ld.n, g_ld.n, s_st.n, mm.n, ve.n, ac.n
                SPR = 2 if debug else 1
                with nc.Block() as blk:
                    @blk.gpsimd
                    def _(g):
                        g.dma_start(out=v2_sb[:], in_=v2_d).then_inc(g_ld.h, 16)
                        g.dma_start(out=b2_sb[:], in_=b2_d.to_broadcast([128, L])).then_inc(g_ld.h, 16)
                    g_ld.n += 32

                    @blk.sync
                    def _(sy):
                        for rb in range(RB):
                            if rb >= 2:
                                sy.wait_ge(ve.h, ve0 + 3 * (rb - 2) + 3)
                            x9.inc(rb, sy.dma_start(out=x_t2[rb % 2][:],
                                                    in_=x_d[rb * 128:(rb + 1) * 128, :]))
                            if rb >= 1:
                                sy.wait_ge(ve.h, ve0 + 3 * (rb - 1) + 2)
                            s9.inc(sy.dma_start(out=s_t2[:],
                                                in_=s_d[rb * 128:(rb + 1) * 128, :]), 16)
                            sy.wait_ge(ve.h, ve0 + 3 * rb + 3)
                            so.inc(rb, sy.dma_start(out=out_d[rb * 128:(rb + 1) * 128, :],
                                                    in_=o_t[rb % 2][:]))
                            if debug:
                                sy.dma_start(out=g_dump[rb * 128:(rb + 1) * 128, :],
                                             in_=g_t[:]).then_inc(s_st.h, 16)
                        if debug:
                            sy.dma_start(out=zdump_d, in_=Z_sb[:]).then_inc(s_st.h, 16)
                            sy.wait_ge(s_st.h, st0 + 16 * (RB + 1))
                        if (RB + 1) // 2 > 0:
                            sy.wait_ge(so.h(0), 16 * ((RB + 1) // 2))
                        if RB // 2 > 0:
                            sy.wait_ge(so.h(1), 16 * (RB // 2))

                    s_st.n += 16 * (RB + 1) if debug else 0

                    @blk.vector
                    def _(v):
                        for rb in range(RB):
                            v.wait_ge(mm.h, mm0 + rb + 1)
                            v.wait_ge(x9.h(rb), 16 * (rb // 2 + 1))
                            v.wait_ge(s9.h, 16 * (rb + 1))
                            v.wait_ge(g_ld.h, gld0 + 32)
                            ve.inc(v.tensor_tensor(f1[:], psO[:], b2_sb[:], ALU.add))
                            v.wait_ge(ac.h, ac0 + rb + 1)
                            nv9 = ve.inc(v.tensor_tensor(g_t[:], s_t2[:], g_t[:], ALU.mult))
                            v.wait_ge(ve.h, nv9)
                            if rb >= 2:
                                v.wait_ge(so.h(rb), 16 * (rb // 2))
                            ve.inc(v.tensor_tensor(o_t[rb % 2][:], x_t2[rb % 2][:], g_t[:], ALU.add))

                    @blk.scalar
                    def _(sc):
                        for rb in range(RB):
                            sc.wait_ge(ve.h, ve0 + 3 * rb + 1)
                            ac.inc(sc.activation(out=g_t[:], in_=f1[:], func=AF.Silu))

                    @blk.tensor
                    def _(t):
                        t.wait_ge(g_ld.h, gld0 + 32)
                        for rb in range(RB):
                            if rb >= 1:
                                t.wait_ge(ve.h, ve0 + 3 * (rb - 1) + 1)
                            last = None
                            for rc in range(8):
                                for hb in range(8):
                                    last = t.matmul(
                                        psO[:, hb * 512:(hb + 1) * 512],
                                        Z_sb[:, rc * rows + rb * 128:rc * rows + (rb + 1) * 128],
                                        v2_sb[:, rc * L + hb * 512:rc * L + (hb + 1) * 512],
                                        start=(rc == 0), stop=(rc == 7))
                            mm.inc(last)

    return nc


def _host_layouts(U1, V1, U2, V2, b1):
    U1t = np.ascontiguousarray(
        np.asarray(U1, np.float32).reshape(32, 128, 8, 128).transpose(2, 1, 0, 3)
    ).reshape(8, 128, 4096)
    arr = np.asarray(V1, np.float32).reshape(8, 128, 172, 128).transpose(2, 1, 0, 3)
    V1t = np.ascontiguousarray(
        arr.reshape(86, 2, 128, 8, 128).transpose(0, 2, 3, 1, 4)
    ).reshape(86, 128, 2048)
    U2t = np.ascontiguousarray(
        np.asarray(U2, np.float32).reshape(86, 128, 8, 128)
    ).reshape(86, 128, 1024)
    V2t = np.ascontiguousarray(
        np.asarray(V2, np.float32).reshape(8, 128, 4096).transpose(1, 0, 2)
    ).reshape(128, 8 * 4096)
    b1t = np.ascontiguousarray(np.asarray(b1, np.float32).reshape(172, 128).T)
    return U1t, V1t, U2t, V2t, b1t


def kernel(L0_attn_x, L0_ffn_U1, L0_ffn_V1, L0_ffn_U2, L0_ffn_V2, L0_ffn_b1, L0_ffn_b2):
    x = np.ascontiguousarray(np.asarray(L0_attn_x, np.float32)).reshape(8192, 4096)
    U1t, V1t, U2t, V2t, b1t = _host_layouts(L0_ffn_U1, L0_ffn_V1, L0_ffn_U2,
                                            L0_ffn_V2, L0_ffn_b1)
    b2 = np.ascontiguousarray(np.asarray(L0_ffn_b2, np.float32)).reshape(1, 4096)
    Dt, Cit = dft_consts()

    rows = 8192 // N_CORES
    nc = build_program(rows, debug=False)
    in_maps = []
    for c in range(N_CORES):
        in_maps.append({
            "x": np.ascontiguousarray(x[c * rows:(c + 1) * rows]),
            "U1t": U1t, "V1t": V1t, "U2t": U2t, "V2t": V2t,
            "b1t": b1t, "b2": b2, "Dt": Dt, "Cit": Cit,
        })
    res = run_bass_kernel_spmd(nc, in_maps, core_ids=list(range(N_CORES)))
    out = np.concatenate([res.results[c]["out"] for c in range(N_CORES)], axis=0)
    return out.reshape(2, 4096, 4096)


# version 07
# speedup vs baseline: 1.0636x; 1.0560x over previous
import numpy as np
import ml_dtypes
from contextlib import ExitStack

import concourse.bass as bass
import concourse.mybir as mybir
from concourse.bass_utils import run_bass_kernel_spmd

F32 = mybir.dt.float32
BF16 = mybir.dt.bfloat16
U32 = mybir.dt.uint32
AF = mybir.ActivationFunctionType
ALU = mybir.AluOpType
AX = mybir.AxisListType
BF = ml_dtypes.bfloat16

L = 4096
NF = 2049
NII = 86
NH_A = 44
NH_B = NII - NH_A
N_CORES = 8


class Sem:
    def __init__(self, ctx, nc, name):
        self.h = ctx.enter_context(nc.semaphore(name))
        self.n = 0

    def inc(self, inst, k=1):
        inst.then_inc(self.h, k)
        self.n += k
        return self.n


class SemPair:

    def __init__(self, ctx, nc, name, n=2):
        self.s = [Sem(ctx, nc, f"{name}{i}") for i in range(n)]

    def inc(self, i, inst, k=16):
        return self.s[i % len(self.s)].inc(inst, k)

    def h(self, i):
        return self.s[i % len(self.s)].h

    def val(self, i):
        return self.s[i % len(self.s)].n


def _row_halves(rows):
    out, o = [], 0
    while o < rows:
        sz = min(512, rows - o)
        out.append((o, sz))
        o += sz
    return out


_CONST_CACHE = {}


def dft_consts():
    if "D" in _CONST_CACHE:
        return _CONST_CACHE["D"], _CONST_CACHE["Ci"]
    t = np.arange(L, dtype=np.float64)
    f = np.arange(NF, dtype=np.float64)
    ang = 2.0 * np.pi * np.outer(t, f) / L
    D = np.empty((L, L), dtype=np.float64)
    D[:, 0:NF] = np.cos(ang)
    D[:, NF:] = np.sin(ang[:, 1:2048])
    D_t = np.ascontiguousarray(
        D.astype(BF).reshape(32, 128, 32, 128).transpose(2, 1, 0, 3)
    ).reshape(32, 128, 4096)

    lf = np.arange(2176, dtype=np.float64)
    a = np.zeros(2176, dtype=np.float64)
    a[0] = 1.0 / L
    a[2048] = 1.0 / L
    a[1:2048] = 2.0 / L
    li = np.arange(L, dtype=np.float64)
    Ci = (a[:, None] * np.cos(2.0 * np.pi * np.outer(lf, li) / L)).astype(BF)
    Ci[2049:, :] = 0
    Ci_t = np.ascontiguousarray(
        Ci.reshape(17, 128, 8, 512).transpose(2, 1, 0, 3)
    ).reshape(8, 128, 17 * 512)
    _CONST_CACHE["D"] = D_t
    _CONST_CACHE["Ci"] = Ci_t
    return D_t, Ci_t


def build_program(rows, debug=False):
    RB = rows // 128
    RH = _row_halves(rows)
    PSW = rows if rows <= 512 else 1024

    nc = bass.Bass("TRN2", target_bir_lowering=False)

    x_d = nc.dram_tensor("x", [rows, L], F32, kind="ExternalInput").ap()
    u1_d = nc.dram_tensor("U1t", [8, 128, 4096], F32, kind="ExternalInput").ap()
    v1_d = nc.dram_tensor("V1t", [NII, 128, 2048], F32, kind="ExternalInput").ap()
    u2_d = nc.dram_tensor("U2t", [NII, 128, 1024], F32, kind="ExternalInput").ap()
    v2_d = nc.dram_tensor("V2t", [128, 8 * L], F32, kind="ExternalInput").ap()
    b1_d = nc.dram_tensor("b1t", [128, 172], F32, kind="ExternalInput").ap()
    b2_d = nc.dram_tensor("b2", [1, L], F32, kind="ExternalInput").ap()
    dt_d = nc.dram_tensor("Dt", [32, 128, 4096], BF16, kind="ExternalInput").ap()
    ci_d = nc.dram_tensor("Cit", [8, 128, 17 * 512], BF16, kind="ExternalInput").ap()

    out_d = nc.dram_tensor("out", [rows, L], F32, kind="ExternalOutput").ap()

    skind = "ExternalOutput" if debug else "Internal"
    xnd_d = nc.dram_tensor("xnd", [rows, 2 * L], BF16, kind=skind).ap()
    cor_d = nc.dram_tensor("corrd", [L, rows], BF16, kind=skind).ap()
    s_d = nc.dram_tensor("sdram", [rows, L], BF16, kind=skind).ap()
    if debug:
        pdump_d = nc.dram_tensor("pdump", [128, 17 * rows], BF16, kind="ExternalOutput").ap()
        zdump_d = nc.dram_tensor("zdump", [128, 8 * rows], BF16, kind="ExternalOutput").ap()
        pp_d = nc.dram_tensor("ppd", [128, 8 * rows], BF16, kind="ExternalOutput").ap()
        g_dump = nc.dram_tensor("gdump", [rows, L], BF16, kind="ExternalOutput").ap()

    with ExitStack() as CTX:
        def sb(name, shape, dtype, ctx=None):
            return (ctx or CTX).enter_context(nc.sbuf_tensor(name, shape, dtype))

        def ps(name, shape, ctx=None):
            return (ctx or CTX).enter_context(nc.psum_tensor(name, shape, F32))

        s_ld = Sem(CTX, nc, "s_ld")
        g_ld = Sem(CTX, nc, "g_ld")
        s_st = Sem(CTX, nc, "s_st")
        mm = Sem(CTX, nc, "mm")
        lx = SemPair(CTX, nc, "lx")
        dl = SemPair(CTX, nc, "dl")
        ul = SemPair(CTX, nc, "ul")
        cl = SemPair(CTX, nc, "cl")
        ct = SemPair(CTX, nc, "ct")
        gs = SemPair(CTX, nc, "gs")
        vl = SemPair(CTX, nc, "vl")
        us = SemPair(CTX, nc, "us")
        x9 = SemPair(CTX, nc, "x9")
        s9 = Sem(CTX, nc, "s9")
        so = SemPair(CTX, nc, "so")
        ve = Sem(CTX, nc, "ve")
        ac = Sem(CTX, nc, "ac")
        gp = Sem(CTX, nc, "gp")

        P_sb = sb("P_sb", [128, 8 * rows], BF16)
        b1_sb = sb("b1_sb", [128, 172], F32)
        riota = sb("riota", [128, 1], U32)
        idx_all = sb("idx_all", [128, 8 * RB], U32)
        w8_all = sb("w8_all", [128, 8 * RB], F32)
        ss = Sem(CTX, nc, "ss")

        with nc.Block() as blk:
            @blk.sync
            def _(sy):
                s_ld.inc(sy.dma_start(out=b1_sb[:], in_=b1_d), 16)

            @blk.gpsimd
            def _(g):
                gp.inc(g.iota(riota[:], [[0, 1]], base=0, channel_multiplier=2 * L))

        with ExitStack() as c1:
            x_t = [sb(f"x_t{i}", [128, L], F32, c1) for i in range(2)]
            xn_t = [sb(f"xn_t{i}", [128, L], BF16, c1) for i in range(2)]
            stats = sb("stats", [128, 48], F32, c1)
            mv = sb("mv", [128, 2], F32, c1)
            veps = sb("veps", [128, 1], F32, c1)
            stdv = sb("stdv", [128, 1], F32, c1)
            rstd = sb("rstd", [128, 1], F32, c1)

            ld0, st0, ve0, ac0 = s_ld.n, s_st.n, ve.n, ac.n
            PPR = 5
            with nc.Block() as blk:
                @blk.sync
                def _(sy):
                    for rb in range(RB):
                        if rb >= 2:
                            sy.wait_ge(ve.h, ve0 + PPR * (rb - 2) + 5)
                            sy.wait_ge(s_st.h, st0 + 32 * rb)
                        lx.inc(rb, sy.dma_start(out=x_t[rb % 2][:],
                                                in_=x_d[rb * 128:(rb + 1) * 128, :]))
                        sy.wait_ge(ve.h, ve0 + PPR * rb + 5)
                        sy.dma_start(out=xnd_d[rb * 128:(rb + 1) * 128, 0:L],
                                     in_=xn_t[rb % 2][:]).then_inc(s_st.h, 16)
                        sy.dma_start(out=xnd_d[rb * 128:(rb + 1) * 128, L:2 * L],
                                     in_=xn_t[rb % 2][:]).then_inc(s_st.h, 16)
                    sy.wait_ge(s_st.h, st0 + 32 * RB)
                s_st.n += 32 * RB

                @blk.vector
                def _(v):
                    for rb in range(RB):
                        v.wait_ge(s_ld.h, ld0)
                        v.wait_ge(lx.h(rb), 16 * (rb // 2 + 1))
                        xt = x_t[rb % 2]
                        for j in range(7):
                            v.bn_stats(stats[:, 6 * j:6 * (j + 1)], xt[:, 512 * j:512 * (j + 1)])
                        n1 = ve.inc(v.bn_stats(stats[:, 42:48], xt[:, 3584:4096]))
                        v.wait_ge(ve.h, n1)
                        n2 = ve.inc(v.bn_aggr(mv[:], stats[:]))
                        v.wait_ge(ve.h, n2)
                        ve.inc(v.tensor_scalar_add(veps[:], mv[:, 1:2], 1e-5))
                        v.wait_ge(ac.h, ac0 + rb + 1)
                        n4 = ve.inc(v.reciprocal(rstd[:], stdv[:]))
                        v.wait_ge(ve.h, n4)
                        ve.inc(v.tensor_scalar(xn_t[rb % 2][:], xt[:], mv[:, 0:1], rstd[:],
                                               op0=ALU.subtract, op1=ALU.mult))

                @blk.scalar
                def _(sc):
                    for rb in range(RB):
                        sc.wait_ge(ve.h, ve0 + PPR * rb + 3)
                        ac.inc(sc.activation(out=stdv[:], in_=veps[:], func=AF.Sqrt))

        with ExitStack() as c2:
            xn_fm = sb("xn_fm", [128, 32 * rows], BF16, c2)
            p_sb = sb("p_sb", [128, 17 * rows], BF16, c2)

            ld0 = s_ld.n
            with nc.Block() as blk:
                @blk.sync
                def _(sy):
                    for tc in range(32):
                        sy.dma_start_transpose(
                            out=xn_fm[:, tc * rows:(tc + 1) * rows],
                            in_=xnd_d[0:rows, tc * 128:(tc + 1) * 128]).then_inc(s_ld.h, 16)
                    sy.wait_ge(s_ld.h, ld0 + 16 * 32)
                s_ld.n += 16 * 32

            with ExitStack() as c3:
                d_buf = [sb(f"d_buf{i}", [128, 4096], BF16, c3) for i in range(2)]
                sqa = sb("sqa", [128, rows], F32, c3)
                sqb = sb("sqb", [128, rows], F32, c3)
                psF = [ps(f"psF{i}", [128, PSW], c3) for i in range(2)]

                fc_order = []
                for c in range(16):
                    fc_order += [c, 16 + c]

                ld0, mm0, ve0, ac0 = s_ld.n, mm.n, ve.n, ac.n
                with nc.Block() as blk:
                    @blk.sync
                    def _(sy):
                        for i, fc in enumerate(fc_order):
                            if i >= 2:
                                sy.wait_ge(mm.h, mm0 + i - 1)
                            dl.inc(i, sy.dma_start(out=d_buf[i % 2][:], in_=dt_d[fc]))

                    @blk.tensor
                    def _(t):
                        for i, fc in enumerate(fc_order):
                            t.wait_ge(dl.h(i), 16 * (i // 2 + 1))
                            if i >= 2:
                                t.wait_ge(ac.h, ac0 + i - 1)
                            pst = psF[i % 2]
                            last = None
                            for tc in range(32):
                                for (ro, rsz) in RH:
                                    last = t.matmul(
                                        pst[:, ro:ro + rsz],
                                        d_buf[i % 2][:, tc * 128:(tc + 1) * 128],
                                        xn_fm[:, tc * rows + ro:tc * rows + ro + rsz],
                                        start=(tc == 0), stop=(tc == 31))
                            mm.inc(last)

                    @blk.scalar
                    def _(sc):
                        for j in range(16):
                            sc.wait_ge(mm.h, mm0 + 2 * j + 2)
                            if j >= 1:
                                sc.wait_ge(ve.h, ve0 + j + 2)
                            ac.inc(sc.activation(out=sqa[:], in_=psF[0][:, 0:rows], func=AF.Square))
                            ac.inc(sc.activation(out=sqb[:], in_=psF[1][:, 0:rows], func=AF.Square))

                    @blk.vector
                    def _(v):
                        for j in range(16):
                            c = j
                            v.wait_ge(ac.h, ac0 + 2 * j + 2)
                            if c == 0:
                                n1 = ve.inc(v.tensor_tensor(p_sb[:, 0:rows], sqa[:], sqb[:], ALU.add))
                                ve.inc(v.tensor_copy(p_sb[:, 16 * rows:17 * rows], sqb[:]))
                                v.wait_ge(ve.h, n1)
                                ve.inc(v.tensor_copy(p_sb[0:1, 0:rows], sqa[0:1, :]))
                            else:
                                ve.inc(v.tensor_tensor(p_sb[:, c * rows:(c + 1) * rows],
                                                       sqa[:], sqb[:], ALU.add))

            with ExitStack() as c5:
                ci_buf = [sb(f"ci_buf{i}", [128, 17 * 512], BF16, c5) for i in range(2)]
                cev = sb("cev", [128, 4 * rows], BF16, c5)
                psC = [ps(f"psC{i}", [128, PSW], c5) for i in range(4)]
                cor_rr = cor_d.rearrange("(lg lc p) r -> lg p lc r", lg=8, lc=4, p=128)

                ld0, st0, mm0, ve0 = s_ld.n, s_st.n, mm.n, ve.n
                with nc.Block() as blk:
                    @blk.sync
                    def _(sy):
                        for lg in range(8):
                            if lg >= 2:
                                sy.wait_ge(mm.h, mm0 + lg - 1)
                            cl.inc(lg, sy.dma_start(out=ci_buf[lg % 2][:], in_=ci_d[lg]))
                            if lg >= 1:
                                sy.wait_ge(ve.h, ve0 + lg)
                                sy.dma_start(out=cor_rr[lg - 1],
                                             in_=cev[:].rearrange("p (lc r) -> p lc r", lc=4))\
                                  .then_inc(s_st.h, 16)
                        sy.wait_ge(ve.h, ve0 + 8)
                        sy.dma_start(out=cor_rr[7],
                                     in_=cev[:].rearrange("p (lc r) -> p lc r", lc=4))\
                          .then_inc(s_st.h, 16)
                        sy.wait_ge(s_st.h, st0 + 16 * 8)
                    s_st.n += 16 * 8

                    @blk.tensor
                    def _(t):
                        for lg in range(8):
                            t.wait_ge(cl.h(lg), 16 * (lg // 2 + 1))
                            if lg >= 1:
                                t.wait_ge(ve.h, ve0 + lg)
                            last = None
                            for cc in range(17):
                                for lc in range(4):
                                    for (ro, rsz) in RH:
                                        last = t.matmul(
                                            psC[lc][:, ro:ro + rsz],
                                            ci_buf[lg % 2][:, cc * 512 + lc * 128:cc * 512 + (lc + 1) * 128],
                                            p_sb[:, cc * rows + ro:cc * rows + ro + rsz],
                                            start=(cc == 0), stop=(cc == 16))
                            mm.inc(last)

                    @blk.vector
                    def _(v):
                        for lg in range(8):
                            v.wait_ge(mm.h, mm0 + lg + 1)
                            if lg >= 1:
                                v.wait_ge(s_st.h, st0 + 16 * lg)
                            for lc in range(3):
                                v.tensor_copy(cev[:, lc * rows:(lc + 1) * rows], psC[lc][:, 0:rows])
                            ve.inc(v.tensor_copy(cev[:, 3 * rows:4 * rows], psC[3][:, 0:rows]))

            with ExitStack() as c35:
                psP = [ps(f"psP{i}", [128, PSW], c35) for i in range(2)]
                u_buf = [sb(f"u_buf{i}", [128, 4096], BF16, c35) for i in range(2)]
                cor_t = [sb(f"cor_t{i}", [128, L], BF16, c35) for i in range(2)]
                v8 = sb("v8", [128, 8], BF16, c35)
                d8 = sb("d8", [128, 8], U32, c35)
                neg0 = sb("neg0", [128, 1], F32, c35)
                e8 = sb("e8", [128, 8], F32, c35)
                esum = sb("esum", [128, 1], F32, c35)
                erin = sb("erin", [128, 1], F32, c35)

                mm0, ve0, ac0 = mm.n, ve.n, ac.n
                ve_evac, ve_mi, ve_neg = {}, {}, {}
                cnt = ve.n
                for i in range(8):
                    cnt += 1
                    ve_evac[i] = cnt
                    if i < RB:
                        cnt += 1
                        cnt += 1
                        ve_mi[i] = cnt
                        cnt += 1
                        cnt += 1
                        ve_neg[i] = cnt
                        cnt += 3
                with nc.Block() as blk:
                    @blk.gpsimd
                    def _(g):
                        for rc in range(8):
                            if rc >= 2:
                                g.wait_ge(mm.h, mm0 + rc - 1)
                            ul.inc(rc, g.dma_start(out=u_buf[rc % 2][:], in_=u1_d[rc]))

                    @blk.tensor
                    def _(t):
                        for rc in range(8):
                            t.wait_ge(ul.h(rc), 16 * (rc // 2 + 1))
                            if rc >= 2:
                                t.wait_ge(ve.h, ve_evac[rc - 2])
                            last = None
                            for hc in range(32):
                                for (ro, rsz) in RH:
                                    last = t.matmul(
                                        psP[rc % 2][:, ro:ro + rsz],
                                        u_buf[rc % 2][:, hc * 128:(hc + 1) * 128],
                                        xn_fm[:, hc * rows + ro:hc * rows + ro + rsz],
                                        start=(hc == 0), stop=(hc == 31))
                            mm.inc(last)

                    @blk.sync
                    def _(sy):
                        for rb in range(RB):
                            if rb >= 2:
                                sy.wait_ge(ve.h, ve_mi[rb - 2])
                            ct.inc(rb, sy.dma_start_transpose(
                                out=cor_t[rb % 2][:],
                                in_=cor_d[:, rb * 128:(rb + 1) * 128]))

                    @blk.vector
                    def _(v):
                        for i in range(8):
                            v.wait_ge(mm.h, mm0 + i + 1)
                            n = ve.inc(v.tensor_copy(P_sb[:, i * rows:(i + 1) * rows],
                                                     psP[i % 2][:, 0:rows]))
                            assert n == ve_evac[i]
                            if i < RB:
                                v.wait_ge(ct.h(i), 16 * (i // 2 + 1))
                                n = ve.inc(v.max(out=v8[:], in_=cor_t[i % 2][:]))
                                v.wait_ge(ve.h, n)
                                n = ve.inc(v.max_index(out=d8[:], in_max=v8[:],
                                                       in_values=cor_t[i % 2][:]))
                                assert n == ve_mi[i]
                                v.wait_ge(ve.h, n)
                                ve.inc(v.tensor_tensor(idx_all[:, 8 * i:8 * (i + 1)], d8[:],
                                                       riota[:].to_broadcast([128, 8]), ALU.add))
                                n = ve.inc(v.tensor_scalar_mul(neg0[:], v8[:, 0:1], -1.0))
                                assert n == ve_neg[i]
                                v.wait_ge(ac.h, ac0 + i + 1)
                                n = ve.inc(v.tensor_reduce(esum[:], e8[:], AX.X, ALU.add))
                                v.wait_ge(ve.h, n)
                                n = ve.inc(v.reciprocal(erin[:], esum[:]))
                                v.wait_ge(ve.h, n)
                                ve.inc(v.tensor_scalar(w8_all[:, 8 * i:8 * (i + 1)], e8[:],
                                                       erin[:], None, op0=ALU.mult))

                    @blk.scalar
                    def _(sc):
                        for rb in range(RB):
                            sc.wait_ge(ve.h, ve_neg[rb])
                            ac.inc(sc.activation(out=e8[:], in_=v8[:], func=AF.Exp,
                                                 bias=neg0[:], scale=1.0))

            if debug:
                st0 = s_st.n
                with nc.Block() as blk:
                    @blk.sync
                    def _(sy):
                        sy.dma_start(out=pdump_d, in_=p_sb[:]).then_inc(s_st.h, 16)
                        sy.dma_start(out=pp_d, in_=P_sb[:]).then_inc(s_st.h, 16)
                        sy.wait_ge(s_st.h, st0 + 32)
                    s_st.n += 32

        with ExitStack() as c7:
            Z_sb = sb("Z_sb", [128, 8 * rows], BF16, c7)

            with ExitStack() as c7a:
                act_sb = sb("act_sb", [128, NH_A * rows], BF16, c7a)
                v1g = [sb(f"v1g{i}", [128, 2048], BF16, c7a) for i in range(2)]
                v1u = [sb(f"v1u{i}", [128, 2048], BF16, c7a) for i in range(2)]
                u2b = [sb(f"u2b{i}", [128, 1024], BF16, c7a) for i in range(2)]
                silu_b = [sb(f"silu_b{i}", [128, rows], BF16, c7a) for i in range(2)]
                rolled = [sb(f"rolled{i}", [128, L], BF16, c7a) for i in range(2)]
                scaled = [sb(f"scaled{i}", [128, L], BF16, c7a) for i in range(2)]
                s_acc = sb("s_acc", [128, L], BF16, c7a)

                def v1_phase(half):
                    base_ii = 0 if half == 0 else NH_A
                    nii = NH_A if half == 0 else NH_B
                    nsteps = nii // 2
                    units = (list(range((RB + 1) // 2)) if half == 0
                             else list(range((RB + 1) // 2, RB)))
                    nu = len(units)
                    cuts = {}
                    for j in range(nu):
                        cuts[(j + 1) * nii // nu - 1] = j
                    ops_ac, ops_ve = [], []
                    for li in range(nii):
                        ops_ac.append(("silu", li))
                        ops_ve.append(("stt", li))
                        if li in cuts:
                            u = cuts[li]
                            for k in range(8):
                                ops_ac.append(("mul", u, k))
                            for k in range(1, 8):
                                ops_ve.append(("add", u, k))
                    ac_val = {op: ac.n + i + 1 for i, op in enumerate(ops_ac)}
                    ve_val = {op: ve.n + i + 1 for i, op in enumerate(ops_ve)}
                    gs_b = [gs.val(0), gs.val(1)]
                    gat_list = [(u, k) for u in range(nu) for k in range(8)]

                    with ExitStack() as pctx:
                        psG = [ps(f"psG{half}_{i}", [128, PSW], pctx) for i in range(2)]
                        psU = [ps(f"psU{half}_{i}", [128, PSW], pctx) for i in range(2)]
                        mm0 = mm.n
                        vl_b = [vl.val(0), vl.val(1)]
                        with nc.Block() as blk:
                            cut_of = {u: c for c, u in cuts.items()}
                            gat_at = {}
                            for u in range(nu):
                                su = min((cut_of[u] + 1) // 2 + 1, nsteps - 1)
                                gat_at.setdefault(su, []).append(u)

                            @blk.gpsimd
                            def _(g):
                                for st in range(nsteps):
                                    if st >= 2:
                                        g.wait_ge(mm.h, mm0 + 2 * (2 * (st - 2) + 1) + 2)
                                    gg = base_ii // 2 + st
                                    vl.inc(st, g.dma_start(out=v1g[st % 2][:], in_=v1_d[gg]))
                                    vl.inc(st, g.dma_start(out=v1u[st % 2][:], in_=v1_d[43 + gg]))
                                    for u in gat_at.get(st, []):
                                        rbg = units[u]
                                        for k in range(8):
                                            if k >= 2:
                                                g.wait_ge(ac.h, ac_val[("mul", u, k - 2)])
                                            elif u >= 1:
                                                g.wait_ge(ac.h, ac_val[("mul", u - 1, k + 6)])
                                            gs.inc(k, g.indirect_dma_start(
                                                out=rolled[k % 2][:], out_offset=None,
                                                in_=xnd_d,
                                                in_offset=bass.IndirectOffsetOnAxis(
                                                    idx_all[:, 8 * rbg + k:8 * rbg + k + 1], 1),
                                                element_offset=rbg * 128 * 2 * L,
                                            ))

                            @blk.tensor
                            def _(t):
                                for li in range(nii):
                                    st, i2 = li // 2, li % 2
                                    t.wait_ge(vl.h(st), vl_b[st % 2] + 32 * (st // 2 + 1))
                                    if li >= 2:
                                        t.wait_ge(ac.h, ac_val[("silu", li - 2)])
                                        t.wait_ge(ve.h, ve_val[("stt", li - 2)])
                                    last = None
                                    for rc in range(8):
                                        for (ro, rsz) in RH:
                                            last = t.matmul(
                                                psG[li % 2][:, ro:ro + rsz],
                                                v1g[st % 2][:, (rc * 2 + i2) * 128:(rc * 2 + i2 + 1) * 128],
                                                P_sb[:, rc * rows + ro:rc * rows + ro + rsz],
                                                start=(rc == 0), stop=(rc == 7))
                                    mm.inc(last)
                                    for rc in range(8):
                                        for (ro, rsz) in RH:
                                            last = t.matmul(
                                                psU[li % 2][:, ro:ro + rsz],
                                                v1u[st % 2][:, (rc * 2 + i2) * 128:(rc * 2 + i2 + 1) * 128],
                                                P_sb[:, rc * rows + ro:rc * rows + ro + rsz],
                                                start=(rc == 0), stop=(rc == 7))
                                    mm.inc(last)

                            @blk.scalar
                            def _(sc):
                                for li in range(nii):
                                    ii = base_ii + li
                                    sc.wait_ge(mm.h, mm0 + 2 * li + 1)
                                    if li >= 2:
                                        sc.wait_ge(ve.h, ve_val[("stt", li - 2)])
                                    n = ac.inc(sc.activation(out=silu_b[li % 2][:],
                                                             in_=psG[li % 2][:, 0:rows],
                                                             func=AF.Silu,
                                                             bias=b1_sb[:, ii:ii + 1], scale=1.0))
                                    assert n == ac_val[("silu", li)]
                                    if li in cuts:
                                        u = cuts[li]
                                        rbg = units[u]
                                        grb = (0 if half == 0 else (RB + 1) // 2) + u
                                        for k in range(8):
                                            sc.wait_ge(gs.h(k), gs_b[k % 2] + 16 * (u * 4 + k // 2 + 1))
                                            if k >= 3:
                                                sc.wait_ge(ve.h, ve_val[("add", u, k - 2)])
                                            elif k in (1, 2) and u >= 1:
                                                sc.wait_ge(ve.h, ve_val[("add", u - 1, 7 if k == 1 else 6)])
                                            if k == 0 and grb >= 1:
                                                sc.wait_ge(ss.h, 16 * grb)
                                            n = ac.inc(sc.mul(
                                                out=(s_acc[:] if k == 0 else scaled[k % 2][:]),
                                                in_=rolled[k % 2][:],
                                                mul=w8_all[:, 8 * rbg + k:8 * rbg + k + 1]))
                                            assert n == ac_val[("mul", u, k)]

                            @blk.vector
                            def _(v):
                                for li in range(nii):
                                    v.wait_ge(mm.h, mm0 + 2 * li + 2)
                                    v.wait_ge(ac.h, ac_val[("silu", li)])
                                    ii = base_ii + li
                                    n = ve.inc(v.scalar_tensor_tensor(
                                        out=act_sb[:, li * rows:(li + 1) * rows],
                                        in0=psU[li % 2][:, 0:rows],
                                        scalar=b1_sb[:, 86 + ii:86 + ii + 1],
                                        in1=silu_b[li % 2][:], op0=ALU.add, op1=ALU.mult))
                                    assert n == ve_val[("stt", li)]
                                    if li in cuts:
                                        u = cuts[li]
                                        for k in range(1, 8):
                                            v.wait_ge(ac.h, ac_val[("mul", u, k)])
                                            if k >= 2:
                                                v.wait_ge(ve.h, ve_val[("add", u, k - 1)])
                                            n = ve.inc(v.tensor_tensor(s_acc[:], s_acc[:],
                                                                       scaled[k % 2][:], ALU.add))
                                            assert n == ve_val[("add", u, k)]

                            @blk.sync
                            def _(sy):
                                for u in range(nu):
                                    rbg = units[u]
                                    sy.wait_ge(ve.h, ve_val[("add", u, 7)])
                                    ss.inc(sy.dma_start(out=s_d[rbg * 128:(rbg + 1) * 128, :],
                                                        in_=s_acc[:]), 16)

                def u2_phase(half):
                    base_ii = 0 if half == 0 else NH_A
                    nii = NH_A if half == 0 else NH_B
                    with ExitStack() as pctx:
                        psZ = [ps(f"psZ{half}_{i}", [128, min(512, rows)], pctx) for i in range(8)]
                        for (ro, rsz) in RH:
                            mm0, ve0 = mm.n, ve.n
                            us_b = [us.val(0), us.val(1)]
                            with nc.Block() as blk:
                                @blk.gpsimd
                                def _(g):
                                    for li in range(nii):
                                        if li >= 2:
                                            g.wait_ge(mm.h, mm0 + li - 1)
                                        us.inc(li, g.dma_start(out=u2b[li % 2][:],
                                                               in_=u2_d[base_ii + li]))

                                @blk.tensor
                                def _(t):
                                    for li in range(nii):
                                        t.wait_ge(us.h(li), us_b[li % 2] + 16 * (li // 2 + 1))
                                        last = None
                                        for rc in range(8):
                                            last = t.matmul(
                                                psZ[rc][:, 0:rsz],
                                                u2b[li % 2][:, rc * 128:(rc + 1) * 128],
                                                act_sb[:, li * rows + ro:li * rows + ro + rsz],
                                                start=(li == 0), stop=(li == nii - 1))
                                        mm.inc(last)

                                @blk.vector
                                def _(v):
                                    v.wait_ge(mm.h, mm0 + nii)
                                    for rc in range(7):
                                        zsl = Z_sb[:, rc * rows + ro:rc * rows + ro + rsz]
                                        if half == 0:
                                            v.tensor_copy(zsl, psZ[rc][:, 0:rsz])
                                        else:
                                            v.tensor_tensor(zsl, zsl, psZ[rc][:, 0:rsz], ALU.add)
                                    zsl = Z_sb[:, 7 * rows + ro:7 * rows + ro + rsz]
                                    if half == 0:
                                        ve.inc(v.tensor_copy(zsl, psZ[7][:, 0:rsz]))
                                    else:
                                        ve.inc(v.tensor_tensor(zsl, zsl, psZ[7][:, 0:rsz], ALU.add))

                v1_phase(0)
                u2_phase(0)
                v1_phase(1)
                u2_phase(1)

            with ExitStack() as c9:
                v2_sb = sb("v2_sb", [128, 8 * L], BF16, c9)
                b2_sb = sb("b2_sb", [128, L], BF16, c9)
                x_t2 = [sb(f"x_t2{i}", [128, L], F32, c9) for i in range(2)]
                s_t2 = sb("s_t2", [128, L], BF16, c9)
                f1 = sb("f1", [128, L], F32, c9)
                g_t = sb("g_t", [128, L], BF16, c9)
                o_t = [sb(f"o_t{i}", [128, L], F32, c9) for i in range(2)]
                psO = ps("psO", [128, L], c9)

                ld0, gld0, st0, mm0, ve0, ac0 = s_ld.n, g_ld.n, s_st.n, mm.n, ve.n, ac.n
                SPR = 2 if debug else 1
                with nc.Block() as blk:
                    @blk.gpsimd
                    def _(g):
                        g.dma_start(out=v2_sb[:], in_=v2_d).then_inc(g_ld.h, 16)
                        g.dma_start(out=b2_sb[:], in_=b2_d.to_broadcast([128, L])).then_inc(g_ld.h, 16)
                    g_ld.n += 32

                    @blk.sync
                    def _(sy):
                        sy.wait_ge(ss.h, 16 * RB)
                        for rb in range(RB):
                            if rb >= 2:
                                sy.wait_ge(ve.h, ve0 + 3 * (rb - 2) + 3)
                            x9.inc(rb, sy.dma_start(out=x_t2[rb % 2][:],
                                                    in_=x_d[rb * 128:(rb + 1) * 128, :]))
                            if rb >= 1:
                                sy.wait_ge(ve.h, ve0 + 3 * (rb - 1) + 2)
                            s9.inc(sy.dma_start(out=s_t2[:],
                                                in_=s_d[rb * 128:(rb + 1) * 128, :]), 16)
                            sy.wait_ge(ve.h, ve0 + 3 * rb + 3)
                            so.inc(rb, sy.dma_start(out=out_d[rb * 128:(rb + 1) * 128, :],
                                                    in_=o_t[rb % 2][:]))
                            if debug:
                                sy.dma_start(out=g_dump[rb * 128:(rb + 1) * 128, :],
                                             in_=g_t[:]).then_inc(s_st.h, 16)
                        if debug:
                            sy.dma_start(out=zdump_d, in_=Z_sb[:]).then_inc(s_st.h, 16)
                            sy.wait_ge(s_st.h, st0 + 16 * (RB + 1))
                        if (RB + 1) // 2 > 0:
                            sy.wait_ge(so.h(0), 16 * ((RB + 1) // 2))
                        if RB // 2 > 0:
                            sy.wait_ge(so.h(1), 16 * (RB // 2))

                    s_st.n += 16 * (RB + 1) if debug else 0

                    @blk.vector
                    def _(v):
                        for rb in range(RB):
                            v.wait_ge(mm.h, mm0 + rb + 1)
                            v.wait_ge(x9.h(rb), 16 * (rb // 2 + 1))
                            v.wait_ge(s9.h, 16 * (rb + 1))
                            v.wait_ge(g_ld.h, gld0 + 32)
                            ve.inc(v.tensor_tensor(f1[:], psO[:], b2_sb[:], ALU.add))
                            v.wait_ge(ac.h, ac0 + rb + 1)
                            nv9 = ve.inc(v.tensor_tensor(g_t[:], s_t2[:], g_t[:], ALU.mult))
                            v.wait_ge(ve.h, nv9)
                            if rb >= 2:
                                v.wait_ge(so.h(rb), 16 * (rb // 2))
                            ve.inc(v.tensor_tensor(o_t[rb % 2][:], x_t2[rb % 2][:], g_t[:], ALU.add))

                    @blk.scalar
                    def _(sc):
                        for rb in range(RB):
                            sc.wait_ge(ve.h, ve0 + 3 * rb + 1)
                            ac.inc(sc.activation(out=g_t[:], in_=f1[:], func=AF.Silu))

                    @blk.tensor
                    def _(t):
                        t.wait_ge(g_ld.h, gld0 + 32)
                        for rb in range(RB):
                            if rb >= 1:
                                t.wait_ge(ve.h, ve0 + 3 * (rb - 1) + 1)
                            last = None
                            for rc in range(8):
                                for hb in range(8):
                                    last = t.matmul(
                                        psO[:, hb * 512:(hb + 1) * 512],
                                        Z_sb[:, rc * rows + rb * 128:rc * rows + (rb + 1) * 128],
                                        v2_sb[:, rc * L + hb * 512:rc * L + (hb + 1) * 512],
                                        start=(rc == 0), stop=(rc == 7))
                            mm.inc(last)

    return nc


def _host_layouts(U1, V1, U2, V2, b1):
    U1t = np.ascontiguousarray(
        np.asarray(U1, np.float32).reshape(32, 128, 8, 128).transpose(2, 1, 0, 3)
    ).reshape(8, 128, 4096)
    arr = np.asarray(V1, np.float32).reshape(8, 128, 172, 128).transpose(2, 1, 0, 3)
    V1t = np.ascontiguousarray(
        arr.reshape(86, 2, 128, 8, 128).transpose(0, 2, 3, 1, 4)
    ).reshape(86, 128, 2048)
    U2t = np.ascontiguousarray(
        np.asarray(U2, np.float32).reshape(86, 128, 8, 128)
    ).reshape(86, 128, 1024)
    V2t = np.ascontiguousarray(
        np.asarray(V2, np.float32).reshape(8, 128, 4096).transpose(1, 0, 2)
    ).reshape(128, 8 * 4096)
    b1t = np.ascontiguousarray(np.asarray(b1, np.float32).reshape(172, 128).T)
    return U1t, V1t, U2t, V2t, b1t


def kernel(L0_attn_x, L0_ffn_U1, L0_ffn_V1, L0_ffn_U2, L0_ffn_V2, L0_ffn_b1, L0_ffn_b2):
    x = np.ascontiguousarray(np.asarray(L0_attn_x, np.float32)).reshape(8192, 4096)
    U1t, V1t, U2t, V2t, b1t = _host_layouts(L0_ffn_U1, L0_ffn_V1, L0_ffn_U2,
                                            L0_ffn_V2, L0_ffn_b1)
    b2 = np.ascontiguousarray(np.asarray(L0_ffn_b2, np.float32)).reshape(1, 4096)
    Dt, Cit = dft_consts()

    rows = 8192 // N_CORES
    nc = build_program(rows, debug=False)
    in_maps = []
    for c in range(N_CORES):
        in_maps.append({
            "x": np.ascontiguousarray(x[c * rows:(c + 1) * rows]),
            "U1t": U1t, "V1t": V1t, "U2t": U2t, "V2t": V2t,
            "b1t": b1t, "b2": b2, "Dt": Dt, "Cit": Cit,
        })
    res = run_bass_kernel_spmd(nc, in_maps, core_ids=list(range(N_CORES)))
    out = np.concatenate([res.results[c]["out"] for c in range(N_CORES)], axis=0)
    return out.reshape(2, 4096, 4096)
